# version 02 (final)
# baseline (speedup 1.0000x reference)
import numpy as np
import concourse.bass as bass
import concourse.mybir as mybir
from concourse.bass_utils import run_bass_kernel_spmd
from concourse.bass_core_rust import IndirectOffsetOnAxis

F32 = mybir.dt.float32
I32 = mybir.dt.int32
AF = mybir.ActivationFunctionType
OP = mybir.AluOpType

NCORE = 8
L_OWN = 2048
HALO = 6
LE = L_OWN + HALO
LE1 = LE + 1
D = 1024
DCH = 8
EFF = 8
HFF = 2048
HCH = 16
CAP = 2048
CPAD = 384
QCH = 3
ED = 2048
ECH = 16
NST = 16
RK = 64
EPS = 1e-5
BIS = 45
SBND = 64.0
NT = 512
LH = 1027
ENGS = ("sync", "tensor", "vector", "scalar", "gpsimd")


def ltiles(lo, hi, step=NT):
    out = []
    c = lo
    while c < hi:
        out.append((c, min(c + step, hi)))
        c = min(c + step, hi)
    return out


class Sem:
    __slots__ = ("h", "n")

    def __init__(self, h):
        self.h = h
        self.n = 0


class PB:
    def __init__(self, nc):
        self.nc = nc
        self.ops = {e: [] for e in ENGS}
        self.gates = {}

    def sem(self, name):
        return Sem(self.nc.semaphore(name).__enter__())

    def emit(self, eng, fn):
        self.ops[eng].append(fn)

    def wait(self, eng, sem, val=None):
        v = sem.n if val is None else val
        if v <= 0:
            return
        self.emit(eng, lambda e, h=sem.h, vv=v: e.wait_ge(h, vv))

    def dma(self, q, out, in_, sem, **kw):
        sem.n += 16
        self.emit(q, lambda e, o=out, i=in_, h=sem.h, k=dict(kw):
                  e.dma_start(out=o, in_=i, **k).then_inc(h, 16))

    def guard(self, eng, key):
        for (sem, cnt) in self.gates.get(key, ()):
            self.wait(eng, sem, cnt)

    def release(self, key, *sems):
        self.gates[key] = tuple((s, s.n) for s in sems)


DBG_FFN = False
DBG_MAMBA = False


def build_program(stage=3, taps=(), nmamba=2, ffn_cut=99):
    nc = bass.Bass("TRN2", target_bir_lowering=False,
                   detect_race_conditions=False)
    taps = list(taps)

    def inp(name, shape, dtype=F32):
        return nc.dram_tensor(name, shape, dtype, kind="ExternalInput").ap()

    x_in = inp("x_in", [D, LE])
    maskrow = inp("maskrow", [1, LE])
    foldmask = inp("foldmask", [128, 8])
    bd16 = inp("bd16", [128, 128])
    sel16 = inp("sel16", [128, EFF])
    ones_r = inp("ones_r", [1, 128])
    ones_c = inp("ones_c", [128, 1])
    qiota = inp("qiota", [128, QCH])
    idmat = inp("idmat", [128, 128])
    WgT = inp("WgT", [128, DCH * EFF])
    W1T = inp("W1T", [EFF, D, HFF])
    W3T = inp("W3T", [EFF, D, HFF])
    W2T = inp("W2T", [EFF, HFF, D])
    nrmw = inp("nrmw", [2, 128, DCH])
    WinT = inp("WinT", [2, D, 2 * ED])
    convw = inp("convw", [2, 128, ECH * 4])
    convb = inp("convb", [2, 128, ECH])
    WxT = inp("WxT", [2, ED, 96])
    WdtT = inp("WdtT", [2, RK, ED])
    bdt_c = inp("bdt_c", [2, 128, ECH])
    Alog_c = inp("Alog_c", [2, 128, ECH * NST])
    Dp_c = inp("Dp_c", [2, 128, ECH])
    WoutT = inp("WoutT", [2, ED, D])

    out_x = nc.dram_tensor("out_x", [D, L_OWN], F32, kind="ExternalOutput").ap()
    dbg = {}
    if DBG_MAMBA:
        for nm in ("yz3h0", "yz4h0", "yz3h1", "yz4h1"):
            dbg[nm] = nc.dram_tensor("dbg_" + nm, [128, 8 * LH], F32,
                                     kind="ExternalOutput").ap()
    if DBG_FFN:
        for nm, shp in (("G", [128, 1024]), ("bc0", [128, 16]),
                        ("idxf", [128, 24]), ("wsl", [128, 192]),
                        ("wh", [LE1, EFF]), ("xntm", [LE1, D]),
                        ("cum8", [8, LE]), ("sloc", [8, LE1]),
                        ("bcol", [128, 16]),
                        ("xgtm", [128, 3072]), ("xgdm", [128, 3072]),
                        ("h1s", [128, 6144]), ("hp", [128, 6144]),
                        ("ydm", [128, 3072]), ("ytm", [128, 3072])):
            dbg[nm] = nc.dram_tensor("dbg_" + nm, shp, F32,
                                     kind="ExternalOutput").ap()
    tap_aps = {}
    for t in taps:
        tap_aps[t] = nc.dram_tensor("tap_" + t, [D, LE], F32,
                                    kind="ExternalOutput").ap()

    def scr(name, shape, dtype=F32):
        return nc.dram_tensor(name, shape, dtype, kind="Internal").ap()

    x_hbm = scr("x_hbm", [D, LE])
    xa_hbm = scr("xa_hbm", [D, LE])
    xn_tm = scr("xn_tm", [LE1, D])
    acc_tm = scr("acc_tm", [LE1, D])
    w_hbm = scr("w_hbm", [LE1, EFF])
    sc_ag_in = scr("sc_ag_in", [EFF, L_OWN])
    sc_ag_out = scr("sc_ag_out", [NCORE * EFF, L_OWN])
    xc_hbm = scr("xc_hbm", [ED, LE])
    dl_hbm = scr("dl_hbm", [ED, LE])
    sz_hbm = scr("sz_hbm", [ED, LE])
    bc_hbm = scr("bc_hbm", [32, LE])
    cr_ag_in = scr("cr_ag_in", [128, 512])
    cr_ag_out = scr("cr_ag_out", [NCORE * 128, 512])

    def sb(name, shape, dtype=F32):
        return nc.alloc_sbuf_tensor(name, shape, dtype).ap()

    PH1 = sb("PH1", [128, 8 * LE])
    PH2 = sb("PH2", [128, 8 * LE])
    xnA = [PH1[:, c * LE:(c + 1) * LE] for c in range(DCH)]
    big3 = PH1[:, 0:8 * LH]
    big4 = PH1[:, 8 * LH:16 * LH]
    big1 = PH2[:, 0:8 * LH]
    big2 = PH2[:, 8 * LH:16 * LH]
    xg_tm = big3[:, 0:QCH * 1024]
    xg_dm = big3[:, QCH * 1024:QCH * 1024 + DCH * CPAD]
    y_dm = big4[:, 0:DCH * CPAD]
    y_tm = big4[:, DCH * CPAD:DCH * CPAD + QCH * 1024]
    h1s = big1[:, 0:HCH * CPAD]
    hp = big2[:, 0:HCH * CPAD]
    wkA = sb("wkA", [128, LE])
    wkB = sb("wkB", [128, LE])
    wkC = sb("wkC", [128, LE])
    wkD = sb("wkD", [128, LE])
    wkE = sb("wkE", [128, LE])
    m_b = wkC
    r_b = wkD
    gj = wkC[:, 0:1024]
    cnt_b = wkD
    WdtT_s = PH2[0:RK, 0:ED]
    WxT_s = PH2[:, ED:ED + ECH * 96]
    srow = PH2[0:1, 3584:3584 + LE]
    rrow = PH2[0:1, 5638:5638 + LE]
    G = PH2[:, 7168:7168 + 1024]
    mask_b = PH2[:, 8216:8216 + LE]
    convB = PH2[:, 10270:10270 + LE + 4]
    wgt3 = PH2[:, 14382:14382 + 2 * 1024]
    wgt1 = PH2[:, 6144:6144 + 2 * NT]
    onesbig = PH1[:, 0:LH]
    onesLE = PH1[:, 0:LE]
    scanLE = PH1[:, LE:2 * LE]
    maskLE = PH1[:, 2 * LE:3 * LE]
    zrow = sb("zrow", [1, D])
    bd16_s = sb("bd16_s", [128, 128])
    sel16_s = sb("sel16_s", [128, EFF])
    ones_r_s = sb("ones_r_s", [1, 128])
    ones_c_s = sb("ones_c_s", [128, 1])
    qiota_s = sb("qiota_s", [128, QCH])
    idm = sb("idm", [128, 128])
    fold_s = sb("fold_s", [128, 8])
    WgT_s = sb("WgT_s", [128, DCH * EFF])
    mrow = sb("mrow", [1, LE])
    small8 = sb("small8", [8, LE1])
    sloc = small8
    w8 = small8
    cum8 = PH2[0:8, 8216:8216 + LE]
    wmask = PH2[0:8, 10270:10270 + LE]
    small2 = sb("small2", [128, LE])
    dtb = small2[0:RK, :]
    BCr = small2[64:96, :]
    bcol = sb("bcol", [128, 16])
    cst = sb("cst", [128, 4])
    idxf = sb("idxf", [128, EFF * QCH])
    idxi = sb("idxi", [128, EFF * QCH], I32)
    wslot = sb("wslot", [128, EFF * QCH * 8])
    t8 = sb("t8", [EFF, 2])
    A_s = sb("A_s", [128, ECH * NST])
    nA_s = sb("nA_s", [128, ECH * NST])
    sumB0 = PH1[:, 2 * LH:2 * LH + 256]
    sumB1 = PH1[:, 2 * LH + 256:2 * LH + 512]
    sumA = PH1[:, 2 * LH + 512:2 * LH + 768]
    crtile = PH1[:, 2 * LH + 768:2 * LH + 1280]
    htmp = PH1[:, 2 * LH + 1280:2 * LH + 1792]
    h_in = sb("h_in", [128, 256])
    scol = sb("scol", [128, 64])
    stt_c = sb("stt_c", [128, 256])
    nbdt = sb("nbdt", [128, ECH])
    tiny = sb("tiny", [128, 64])
    nrmw_s = sb("nrmw_s", [128, DCH])
    convw_s = sb("convw_s", [128, ECH * 4])
    convb_s = sb("convb_s", [128, ECH])
    Dp_s = sb("Dp_s", [128, ECH])
    Alog_s = sb("Alog_s", [128, ECH * NST])
    bdt_s = sb("bdt_s", [128, ECH])

    psA = nc.alloc_psum_tensor("psA", [128, NT], F32).ap()
    psB = nc.alloc_psum_tensor("psB", [128, NT], F32).ap()
    psT = nc.alloc_psum_tensor("psT", [128, NT], F32).ap()
    psC = [nc.alloc_psum_tensor(f"psC{i}", [128, NT], F32).ap()
           for i in range(5)]
    psR = [psT, psA, psB, psC[4]]

    p = PB(nc)
    s_ld = p.sem("s_ld")
    s_gd = p.sem("s_gd")
    s_sd = p.sem("s_sd")
    s_tm = p.sem("s_tm")
    s_vm = p.sem("s_vm")
    s_sm = p.sem("s_sm")
    s_gm = p.sem("s_gm")
    s_cc = p.sem("s_cc")
    s_out = p.sem("s_out")
    ALLS = (s_ld, s_gd, s_sd, s_tm, s_vm, s_sm, s_gm, s_cc)

    def mm(ps, lhsT, rhs, start, stop, sem=None):
        if sem is not None:
            sem.n += 1
            p.emit("tensor", lambda e, o=ps, l=lhsT, r=rhs, a=start, b=stop,
                   h=sem.h: e.matmul(o, l, r, start=a, stop=b).then_inc(h, 1))
        else:
            p.emit("tensor", lambda e, o=ps, l=lhsT, r=rhs, a=start, b=stop:
                   e.matmul(o, l, r, start=a, stop=b))

    def tr(ps, in_, sem=None, base=0):
        np_ = in_.shape[0]
        ident = idm[base:base + np_, base:base + np_]
        if sem is not None:
            sem.n += 1
            p.emit("tensor", lambda e, o=ps, i=in_, m=ident, h=sem.h:
                   e.transpose(o, i, m).then_inc(h, 1))
        else:
            p.emit("tensor", lambda e, o=ps, i=in_, m=ident:
                   e.transpose(o, i, m))

    def vo(name, *args, sem=None, eng="vector", **kw):
        if sem is not None:
            sem.n += 1
            p.emit(eng, lambda e, a=args, k=dict(kw), h=sem.h:
                   getattr(e, name)(*a, **k).then_inc(h, 1))
        else:
            p.emit(eng, lambda e, a=args, k=dict(kw):
                   getattr(e, name)(*a, **k))

    def go(name, *args, sem=None, **kw):
        vo(name, *args, sem=sem, eng="gpsimd", **kw)

    def act(out, in_, func, bias=0.0, scale=1.0, accum_out=None, sem=None):
        if sem is not None:
            sem.n += 1
            p.emit("scalar", lambda e, o=out, i=in_, f=func, b=bias, s=scale,
                   ac=accum_out, h=sem.h:
                   e.activation(o, i, f, bias=b, scale=s, accum_out=ac)
                   .then_inc(h, 1))
        else:
            p.emit("scalar", lambda e, o=out, i=in_, f=func, b=bias, s=scale,
                   ac=accum_out:
                   e.activation(o, i, f, bias=b, scale=s, accum_out=ac))

    def scp(out, in_, sem=None):
        if sem is not None:
            sem.n += 1
            p.emit("scalar", lambda e, o=out, i=in_, h=sem.h:
                   e.copy(o, i).then_inc(h, 1))
        else:
            p.emit("scalar", lambda e, o=out, i=in_: e.copy(o, i))

    def ind_dma(out, out_off, in_, in_off, add=False):
        s_gd.n += 16
        cop = OP.add if add else OP.bypass
        p.emit("gpsimd", lambda e, o=out, oo=out_off, i=in_, io=in_off,
               c=cop, h=s_gd.h: e.indirect_dma_start(
                   out=o, out_offset=oo, in_=i, in_offset=io,
                   compute_op=c).then_inc(h, 16))

    def barrier():
        for e in ENGS:
            for s in ALLS:
                p.wait(e, s)

    p.dma("sync", mrow, maskrow, s_ld)
    p.dma("sync", bd16_s, bd16, s_ld)
    p.dma("sync", sel16_s, sel16, s_ld)
    p.dma("sync", ones_r_s, ones_r, s_ld)
    p.dma("sync", ones_c_s, ones_c, s_ld)
    p.dma("sync", qiota_s, qiota, s_ld)
    p.dma("sync", idm, idmat, s_ld)
    p.dma("sync", fold_s, foldmask, s_ld)
    p.dma("sync", WgT_s, WgT, s_ld)
    p.wait("vector", s_ld)
    vo("memset", cst[:, 0:1], EPS)
    vo("memset", cst[:, 1:2], -0.5)
    vo("memset", cst[:, 2:3], -1.0)
    vo("memset", cst[:, 3:4], 0.0)
    vo("memset", onesbig, 1.0)
    vo("memset", zrow, 0.0)
    vo("memset", sloc[:, LE:LE1], 0.0, sem=s_vm)
    p.wait("gpsimd", s_vm)
    p.dma("gpsimd", xn_tm[LE:LE1, :], zrow, s_gd)
    p.dma("gpsimd", w_hbm[LE:LE1, :], zrow[0:1, 0:EFF], s_gd)
    p.dma("gpsimd", x_hbm, x_in, s_gd)

    def ln_stats_apply(src_hbm, extra_hbm=None, rms_li=None):
        p.wait("sync", s_gd)
        for c in range(DCH):
            p.guard("sync", ("xnA", c))
            p.dma("sync", xnA[c], src_hbm[c * 128:(c + 1) * 128, :], s_ld)
            if extra_hbm is not None:
                p.guard("sync", ("wkA",))
                p.dma("sync", wkA, extra_hbm[c * 128:(c + 1) * 128, :], s_ld)
                p.wait("vector", s_ld)
                vo("tensor_add", xnA[c], xnA[c], wkA, sem=s_vm)
                p.release(("wkA",), s_vm)
                p.wait("gpsimd", s_vm)
                p.dma("gpsimd", x_hbm[c * 128:(c + 1) * 128, :], xnA[c], s_gd)
        p.wait("vector", s_ld)
        p.wait("scalar", s_ld)
        if extra_hbm is not None:
            p.wait("scalar", s_vm)
            p.wait("vector", s_vm)
        for c in range(DCH):
            p.guard("scalar", ("wkB",))
            act(wkB, xnA[c], AF.Square, sem=s_sm)
            if rms_li is None:
                if c == 0:
                    vo("tensor_copy", wkC, xnA[0], sem=s_vm)
                else:
                    vo("tensor_add", wkC, wkC, xnA[c], sem=s_vm)
            p.wait("vector", s_sm)
            if c == 0:
                vo("tensor_copy", wkD, wkB, sem=s_vm)
            else:
                vo("tensor_add", wkD, wkD, wkB, sem=s_vm)
            p.release(("wkB",), s_vm)
        p.wait("tensor", s_vm)
        tl = ltiles(0, LE)
        p.guard("tensor", ("psC_ln",))
        if rms_li is None:
            for i, (a, b) in enumerate(tl):
                mm(psC[i][0:1, 0:b - a], ones_c_s, wkC[:, a:b], True, True,
                   sem=(s_tm if i == len(tl) - 1 else None))
            p.wait("vector", s_tm)
            for i, (a, b) in enumerate(tl):
                vo("tensor_scalar_mul", srow[0:1, a:b], psC[i][0:1, 0:b - a],
                   1.0 / D, sem=(s_vm if i == len(tl) - 1 else None))
            p.wait("tensor", s_vm)
        for i, (a, b) in enumerate(tl):
            mm(psC[i][0:1, 0:b - a], ones_c_s, wkD[:, a:b], True, True,
               sem=(s_tm if i == len(tl) - 1 else None))
        p.wait("vector", s_tm)
        for i, (a, b) in enumerate(tl):
            vo("tensor_scalar_mul", rrow[0:1, a:b], psC[i][0:1, 0:b - a],
               1.0 / D, sem=(s_vm if i == len(tl) - 1 else None))
        if rms_li is None:
            vo("tensor_mul", wkE[0:1, 0:LE], srow, srow)
            vo("tensor_sub", rrow, rrow, wkE[0:1, 0:LE], sem=s_vm)
        p.wait("scalar", s_vm)
        act(rrow, rrow, AF.Ln, bias=cst[0:1, 0:1])
        act(rrow, rrow, AF.Exp, scale=cst[0:1, 1:2], sem=s_sm)
        p.wait("vector", s_sm)
        vo("tensor_mul", rrow, rrow, mrow, sem=s_vm)
        p.wait("tensor", s_vm)
        if rms_li is None:
            for i, (a, b) in enumerate(tl):
                mm(psC[i][:, 0:b - a], ones_r_s, srow[0:1, a:b], True, True,
                   sem=(s_tm if i == len(tl) - 1 else None))
            p.wait("vector", s_tm)
            for i, (a, b) in enumerate(tl):
                vo("tensor_copy", m_b[:, a:b], psC[i][:, 0:b - a],
                   sem=(s_vm if i == len(tl) - 1 else None))
            p.wait("tensor", s_vm)
        for i, (a, b) in enumerate(tl):
            mm(psC[i][:, 0:b - a], ones_r_s, rrow[0:1, a:b], True, True,
               sem=(s_tm if i == len(tl) - 1 else None))
        p.wait("vector", s_tm)
        for i, (a, b) in enumerate(tl):
            vo("tensor_copy", r_b[:, a:b], psC[i][:, 0:b - a],
               sem=(s_vm if i == len(tl) - 1 else None))
        p.release(("psC_ln",), s_vm)
        p.wait("tensor", s_vm)
        for c in range(DCH):
            if rms_li is None:
                vo("tensor_sub", xnA[c], xnA[c], m_b)
                vo("tensor_mul", xnA[c], xnA[c], r_b, sem=s_vm)
            else:
                vo("scalar_tensor_tensor", xnA[c], xnA[c],
                   nrmw_s[:, c:c + 1], r_b, OP.mult, OP.mult, sem=s_vm)
        p.release(("m_b r_b",), s_vm)

    def zero_acc():
        zbuf = PH1[:, 3 * LH:3 * LH + 1024]
        p.wait("vector", s_tm)
        p.wait("vector", s_sm)
        vo("memset", zbuf, 0.0, sem=s_vm)
        p.guard("gpsimd", ("acc",))
        p.wait("gpsimd", s_vm)
        for lt in range(17):
            a = lt * 128
            w = min(128, LE1 - a)
            p.dma("gpsimd", acc_tm[a:a + w, :], zbuf[0:w, :], s_gd)
        p.release(("zbuf_used",), s_gd)

    def ffn_sublayer(first, last):
        cut_out = [False]
        ln_stats_apply(x_hbm, extra_hbm=(None if first else xa_hbm))
        tl = ltiles(0, LE)

        p.wait("tensor", s_vm)
        for c in range(DCH):
            for i, (a, b) in enumerate(tl):
                mm(psC[i][0:EFF, 0:b - a], WgT_s[:, c * EFF:(c + 1) * EFF],
                   xnA[c][:, a:b], c == 0, c == DCH - 1,
                   sem=(s_tm if (c == DCH - 1 and i == len(tl) - 1) else None))
        p.wait("vector", s_tm)
        for i, (a, b) in enumerate(tl):
            vo("tensor_copy", sloc[:, a:b], psC[i][0:EFF, 0:b - a],
               sem=(s_vm if i == len(tl) - 1 else None))
        if ffn_cut <= 10:
            return
        p.wait("scalar", s_vm)
        p.dma("scalar", sc_ag_in, sloc[:, HALO:LE], s_sd)
        p.wait("gpsimd", s_sd)
        s_cc.n += 1
        p.emit("gpsimd", lambda e, h=s_cc.h: e.collective_compute(
            "AllGather", OP.bypass, replica_groups=[list(range(NCORE))],
            ins=[sc_ag_in], outs=[sc_ag_out]).then_inc(h, 1))
        p.wait("gpsimd", s_cc)
        gsrc = (sc_ag_out.rearrange("(r e) (hh w) -> r e hh w", e=EFF, hh=2)
                .transpose([1, 0, 2, 3]))
        for ee in range(EFF):
            p.dma("gpsimd", G[ee * 16:(ee + 1) * 16, :], gsrc[ee], s_gd)

        if ffn_cut <= 0:
            return
        if ffn_cut <= 11:
            return
        p.wait("tensor", s_vm)
        for lt in range(16 if ffn_cut == 12 else 17):
            a = lt * 128
            w = min(128, LE - a)
            p.guard("vector", ("tmrow",))
            for c in range(DCH):
                slot = c % 4
                p.guard("tensor", ("psT", slot))
                tr(psR[slot][0:w, 0:128], xnA[c][:, a:a + w], sem=s_tm)
                p.wait("vector", s_tm)
                vo("tensor_copy", wkA[0:w, c * 128:(c + 1) * 128],
                   psR[slot][0:w, 0:128], sem=s_vm)
                p.release(("psT", slot), s_vm)
            p.wait("sync", s_vm)
            p.dma("sync", xn_tm[a:a + w, :], wkA[0:w, 0:D], s_ld)
            p.release(("tmrow",), s_ld)
            p.release(("wkA",), s_ld)

        if ffn_cut <= 12:
            return
        p.wait("vector", s_gd)
        p.guard("vector", ("m_b r_b",))
        if DBG_FFN:
            p.wait("gpsimd", s_gd)
            p.dma("gpsimd", dbg["G"], G, s_out)
        p.wait("vector", s_tm)
        vo("memset", onesLE, 1.0)
        vo("memset", bcol[:, 0:1], -SBND)
        vo("memset", bcol[:, 1:2], SBND)
        vo("memset", bcol[:, 2:3], 0.0, sem=s_vm)
        p.wait("tensor", s_vm)
        for it in range(BIS):
            p.wait("vector", s_vm)
            vo("tensor_scalar", gj, G, bcol[:, 2:3], None, OP.is_gt)
            vo("tensor_tensor_scan", scanLE[:, 0:1024], onesLE[:, 0:1024],
               gj, cst[:, 3:4], OP.mult, OP.add, sem=s_vm)
            p.wait("tensor", s_vm)
            mm(psT[:, 0:1], bd16_s, scanLE[:, 1023:1024], True, True,
               sem=s_tm)
            vo("tensor_sub", bcol[:, 4:5], bcol[:, 2:3], bcol[:, 0:1],
               sem=s_vm)
            vo("tensor_sub", bcol[:, 9:10], bcol[:, 2:3], bcol[:, 1:2],
               sem=s_vm)
            p.wait("vector", s_tm)
            vo("tensor_scalar", bcol[:, 3:4], psT[:, 0:1], float(CAP), None,
               OP.is_gt, sem=s_vm)
            vo("tensor_scalar", bcol[:, 5:6], psT[:, 0:1], float(CAP), None,
               OP.is_le, sem=s_vm)
            p.wait("vector", s_vm)
            vo("scalar_tensor_tensor", bcol[:, 0:1], bcol[:, 4:5],
               bcol[:, 3:4], bcol[:, 0:1], OP.mult, OP.add, sem=s_vm)
            vo("scalar_tensor_tensor", bcol[:, 1:2], bcol[:, 9:10],
               bcol[:, 5:6], bcol[:, 1:2], OP.mult, OP.add, sem=s_vm)
            p.wait("vector", s_vm)
            vo("tensor_add", bcol[:, 2:3], bcol[:, 0:1], bcol[:, 1:2],
               sem=s_vm)
            p.wait("vector", s_vm)
            vo("tensor_scalar_mul", bcol[:, 2:3], bcol[:, 2:3], 0.5,
               sem=s_vm)
            if DBG_FFN and it == 0:
                p.wait("gpsimd", s_vm)
                p.dma("gpsimd", dbg["bc0"], bcol, s_out)
        p.wait("vector", s_vm)
        vo("tensor_copy", bcol[:, 2:3], bcol[:, 1:2], sem=s_vm)
        vo("tensor_scalar_mul", bcol[:, 6:7], bcol[:, 1:2], -1.0, sem=s_vm)
        p.wait("vector", s_vm)
        p.wait("scalar", s_vm)
        act(gj, G, AF.Exp, bias=bcol[:, 6:7], sem=s_sm)
        p.wait("vector", s_sm)
        vo("tensor_scalar", cnt_b[:, 0:1024], G, bcol[:, 2:3], None,
           OP.is_gt)
        vo("tensor_mul", gj, gj, cnt_b[:, 0:1024])
        vo("tensor_tensor_scan", scanLE[:, 0:1024], onesLE[:, 0:1024],
           gj, cst[:, 3:4], OP.mult, OP.add, sem=s_vm)
        p.wait("tensor", s_vm)
        mm(psT[:, 0:1], bd16_s, scanLE[:, 1023:1024], True, True, sem=s_tm)
        p.wait("vector", s_tm)
        vo("tensor_copy", bcol[:, 7:8], psT[:, 0:1], sem=s_vm)
        p.wait("vector", s_vm)
        vo("reciprocal", bcol[:, 8:9], bcol[:, 7:8], sem=s_vm)
        p.wait("tensor", s_vm)
        mm(psT[0:EFF, 0:3], sel16_s, bcol[:, 6:9], True, True, sem=s_tm)
        p.wait("vector", s_tm)
        vo("tensor_copy", t8[:, 0:1], psT[0:EFF, 0:1])
        vo("tensor_copy", t8[:, 1:2], psT[0:EFF, 2:3], sem=s_vm)
        p.release(("psT", 0), s_vm)
        p.wait("vector", s_vm)
        p.wait("vector", s_ld)
        vo("memset", onesbig[0:EFF, :], 1.0)
        vo("tensor_scalar_add", sloc[:, 0:LE], sloc[:, 0:LE], t8[:, 0:1],
           sem=s_vm)
        vo("tensor_scalar", wmask, sloc[:, 0:LE], 0.0, None, OP.is_gt,
           sem=s_vm)
        vo("tensor_tensor_scan", cum8[:, 0:LH], onesbig[0:EFF, 0:LH],
           wmask[:, 0:LH], 0.0, OP.mult, OP.add, sem=s_vm)
        p.wait("vector", s_vm)
        vo("tensor_tensor_scan", cum8[:, LH:LE], onesbig[0:EFF, 0:LH],
           wmask[:, LH:LE], cum8[:, LH - 1:LH], OP.mult, OP.add, sem=s_vm)
        p.wait("scalar", s_vm, s_vm.n - 2)
        act(w8[:, 0:LE], sloc[:, 0:LE], AF.Exp, sem=s_sm)
        p.wait("vector", s_sm)
        vo("tensor_scalar_mul", w8[:, 0:LE], w8[:, 0:LE], t8[:, 1:2])
        vo("tensor_scalar_mul", w8[:, 0:LE], w8[:, 0:LE], 0.5)
        vo("tensor_mul", w8[:, 0:LE], w8[:, 0:LE], wmask, sem=s_vm)
        p.wait("tensor", s_vm)
        for lt in range(17):
            a = lt * 128
            w = min(128, LE - a)
            p.guard("tensor", ("psT", 0))
            tr(psT[0:w, 0:EFF], w8[:, a:a + w], sem=s_tm)
            p.guard("vector", ("wrow",))
            p.wait("vector", s_tm)
            vo("tensor_copy", wkB[0:w, 0:EFF], psT[0:w, 0:EFF], sem=s_vm)
            p.release(("psT", 0), s_vm)
            p.wait("scalar", s_vm)
            p.dma("scalar", w_hbm[a:a + w, :], wkB[0:w, 0:EFF], s_sd)
            p.release(("wrow",), s_sd)
        for e in range(EFF):
            p.wait("tensor", s_vm)
            p.guard("scalar", ("srowst",))
            p.wait("scalar", s_vm)
            p.dma("scalar", srow, cum8[e:e + 1, :], s_sd)
            p.wait("tensor", s_sd)
            for i, (a, b) in enumerate(tl):
                pse = psA if i % 2 == 0 else psB
                p.guard("tensor", ("psAB", i % 2))
                mm(pse[:, 0:b - a], ones_r_s, srow[0:1, a:b], True, True,
                   sem=s_tm)
                p.wait("vector", s_tm)
                vo("tensor_copy", cnt_b[:, a:b], pse[:, 0:b - a], sem=s_vm)
                p.release(("psAB", i % 2), s_vm)
            p.release(("srowst",), s_tm)
            for qc in range(QCH):
                col = e * QCH + qc
                vo("tensor_scalar", maskLE, cnt_b, qiota_s[:, qc:qc + 1],
                   None, OP.is_le)
                vo("tensor_tensor_scan", scanLE, onesLE, maskLE,
                   cst[:, 3:4], OP.mult, OP.add, sem=s_vm)
                p.wait("vector", s_vm)
                vo("tensor_copy", idxf[:, col:col + 1],
                   scanLE[:, LE - 1:LE],
                   sem=(s_vm if qc == QCH - 1 else None))
        p.wait("vector", s_vm)
        vo("tensor_copy", idxi, idxf, sem=s_vm)
        p.wait("gpsimd", s_vm)
        p.wait("gpsimd", s_sd)
        for e in range(EFF):
            for qc in range(QCH):
                col = e * QCH + qc
                ind_dma(wslot[:, col * 8:(col + 1) * 8], None, w_hbm,
                        IndirectOffsetOnAxis(idxi[:, col:col + 1], 0))

        if ffn_cut <= 2:
            return
        barrier()
        if DBG_FFN:
            p.dma("gpsimd", dbg["idxf"], idxf, s_out)
            p.dma("gpsimd", dbg["wsl"], wslot, s_out)
            p.dma("gpsimd", dbg["wh"], w_hbm, s_out)
            p.dma("gpsimd", dbg["xntm"], xn_tm, s_out)
            p.dma("gpsimd", dbg["cum8"], cum8, s_out)
            p.dma("gpsimd", dbg["sloc"], sloc, s_out)
            p.dma("gpsimd", dbg["bcol"], bcol, s_out)
        for e in range(EFF):
            p.guard("gpsimd", ("xg_tm",))
            for qc in range(QCH):
                col = e * QCH + qc
                ind_dma(xg_tm[:, qc * 1024:(qc + 1) * 1024], None, xn_tm,
                        IndirectOffsetOnAxis(idxi[:, col:col + 1], 0))
            if DBG_FFN and e == 0:
                p.dma("gpsimd", dbg["xgtm"], xg_tm, s_out)
            p.wait("tensor", s_gd)
            p.guard("tensor", ("xg_dm",))
            for qc in range(QCH):
                for c in range(DCH):
                    slot = c % 4
                    p.guard("tensor", ("psT", slot))
                    tr(psR[slot][:, 0:128],
                       xg_tm[:, qc * 1024 + c * 128:qc * 1024 + (c + 1) * 128],
                       sem=s_tm)
                    p.wait("scalar", s_tm)
                    scp(xg_dm[:, c * CPAD + qc * 128:
                              c * CPAD + (qc + 1) * 128],
                        psR[slot][:, 0:128], sem=s_sm)
                    p.release(("psT", slot), s_sm)
            p.release(("xg_tm",), s_sm)
            if DBG_FFN and e == 0:
                p.wait("gpsimd", s_sm)
                p.dma("gpsimd", dbg["xgdm"], xg_dm, s_out)
            p.wait("tensor", s_sm)
            for wi, wt_hbm in ((0, W1T), (1, W3T)):
                for ms in range(4):
                    for k in range(DCH):
                        p.guard("sync", ("wgt1", k % 2))
                        wb = wgt1[:, (k % 2) * NT:(k % 2) * NT + NT]
                        p.dma("sync", wb,
                              wt_hbm[e, k * 128:(k + 1) * 128,
                                     ms * 512:(ms + 1) * 512], s_ld)
                        p.wait("tensor", s_ld)
                        for m in range(4):
                            mm(psC[m][:, 0:CPAD],
                               wb[:, m * 128:(m + 1) * 128],
                               xg_dm[:, k * CPAD:(k + 1) * CPAD],
                               k == 0, k == DCH - 1,
                               sem=(s_tm if m == 3 else None))
                        p.release(("wgt1", k % 2), s_tm)
                    if wi == 0:
                        p.wait("scalar", s_tm)
                        for m in range(4):
                            mc = ms * 4 + m
                            act(h1s[:, mc * CPAD:(mc + 1) * CPAD],
                                psC[m][:, 0:CPAD], AF.Silu, sem=s_sm)
                        p.wait("tensor", s_sm)
                    else:
                        p.wait("vector", s_tm)
                        for m in range(4):
                            mc = ms * 4 + m
                            vo("scalar_tensor_tensor",
                               hp[:, mc * CPAD:(mc + 1) * CPAD],
                               psC[m][:, 0:CPAD], 1.0,
                               h1s[:, mc * CPAD:(mc + 1) * CPAD],
                               OP.mult, OP.mult, sem=s_vm)
                        p.wait("tensor", s_vm)
            if DBG_FFN and e == 0:
                p.wait("gpsimd", s_sm)
                p.wait("gpsimd", s_vm)
                p.dma("gpsimd", dbg["h1s"], h1s, s_out)
                p.dma("gpsimd", dbg["hp"], hp, s_out)
            allps = psC + [psA, psB, psT]
            for kh in range(HCH):
                p.guard("sync", ("wgt3", kh % 2))
                wb = wgt3[:, (kh % 2) * 1024:(kh % 2) * 1024 + 1024]
                p.dma("sync", wb, W2T[e, kh * 128:(kh + 1) * 128, :], s_ld)
                p.wait("tensor", s_ld)
                for cm in range(DCH):
                    mm(allps[cm][:, 0:CPAD], wb[:, cm * 128:(cm + 1) * 128],
                       hp[:, kh * CPAD:(kh + 1) * CPAD],
                       kh == 0, kh == HCH - 1,
                       sem=(s_tm if cm == DCH - 1 else None))
                p.release(("wgt3", kh % 2), s_tm)
            p.wait("vector", s_tm)
            for cm in range(DCH):
                vo("tensor_copy", y_dm[:, cm * CPAD:(cm + 1) * CPAD],
                   allps[cm][:, 0:CPAD],
                   sem=(s_vm if cm == DCH - 1 else None))
            p.wait("tensor", s_vm)
            for qc in range(QCH):
                for c in range(DCH):
                    slot = c % 4
                    p.guard("tensor", ("psT", slot))
                    tr(psR[slot][:, 0:128],
                       y_dm[:, c * CPAD + qc * 128:c * CPAD + (qc + 1) * 128],
                       sem=s_tm)
                    p.wait("vector", s_tm)
                    vo("tensor_copy",
                       y_tm[:, qc * 1024 + c * 128:qc * 1024 + (c + 1) * 128],
                       psR[slot][:, 0:128], sem=s_vm)
                    p.release(("psT", slot), s_vm)
                col = e * QCH + qc
                vo("tensor_scalar_mul", y_tm[:, qc * 1024:(qc + 1) * 1024],
                   y_tm[:, qc * 1024:(qc + 1) * 1024],
                   wslot[:, col * 8 + e:col * 8 + e + 1], sem=s_vm)
            p.wait("gpsimd", s_vm)
            if DBG_FFN and e == 0:
                p.dma("gpsimd", dbg["ydm"], y_dm, s_out)
                p.dma("gpsimd", dbg["ytm"], y_tm, s_out)
            for qc in range(QCH):
                col = e * QCH + qc
                p.wait("gpsimd", s_gd)
                ind_dma(acc_tm, IndirectOffsetOnAxis(idxi[:, col:col + 1], 0),
                        y_tm[:, qc * 1024:(qc + 1) * 1024], None, add=True)
            p.release(("y_work",), s_gd)

        if ffn_cut <= 3:
            return
        barrier()
        for lt in range(17):
            a = lt * 128
            w = min(128, LE - a)
            tb = (wkA if lt % 2 == 0 else wkB)[:, 0:1024]
            p.guard("sync", ("accrow", lt % 2))
            p.dma("sync", tb[0:w, :], acc_tm[a:a + w, :], s_ld)
            p.wait("tensor", s_ld)
            for c in range(DCH):
                slot = c % 4
                p.guard("tensor", ("psT", slot))
                tr(psR[slot][0:128, 0:w],
                   tb[0:w, c * 128:(c + 1) * 128], sem=s_tm)
                p.wait("vector", s_tm)
                vo("tensor_copy", xnA[c][:, a:a + w],
                   psR[slot][:, 0:w], sem=s_vm)
                p.release(("psT", slot), s_vm)
            p.release(("accrow", lt % 2), s_vm)
        p.release(("acc",), s_vm)
        for c in range(DCH):
            p.guard("sync", ("wkE",))
            p.dma("sync", wkE, x_hbm[c * 128:(c + 1) * 128, :], s_ld)
            p.wait("vector", s_ld)
            vo("tensor_add", xnA[c], xnA[c], wkE, sem=s_vm)
            p.release(("wkE",), s_vm)
            p.wait("gpsimd", s_vm)
            if last:
                p.dma("gpsimd", out_x[c * 128:(c + 1) * 128, :],
                      xnA[c][:, HALO:LE], s_out)
            else:
                p.dma("gpsimd", x_hbm[c * 128:(c + 1) * 128, :], xnA[c],
                      s_gd)
            p.release(("xnA", c), s_gd, s_out)

    def mamba_layer(li):
        sst = 3 + 3 * li
        cut = 2050 + 3 * li
        p.dma("sync", nrmw_s, nrmw[li], s_ld)
        p.dma("sync", convw_s, convw[li], s_ld)
        p.dma("sync", convb_s, convb[li], s_ld)
        p.dma("sync", Dp_s, Dp_c[li], s_ld)
        p.dma("sync", Alog_s, Alog_c[li], s_ld)
        p.dma("sync", bdt_s, bdt_c[li], s_ld)
        ln_stats_apply(xa_hbm, rms_li=li)

        p.wait("sync", s_vm)
        p.dma("sync", WdtT_s, WdtT[li], s_ld)
        p.dma("sync", WxT_s.rearrange("pp (m w) -> pp m w", m=ECH),
              WxT[li].rearrange("(m pp) w -> m pp w", pp=128)
              .transpose([1, 0, 2]), s_ld)
        p.dma("sync", mask_b, maskrow.broadcast_to([128, LE]), s_ld)
        p.wait("scalar", s_ld)
        act(nA_s, Alog_s, AF.Exp, sem=s_sm)
        p.wait("vector", s_sm)
        p.wait("vector", s_ld)
        vo("tensor_scalar_mul", A_s, nA_s, -1.0)
        vo("tensor_scalar_mul", nbdt, bdt_s, -1.0)
        vo("memset", convB[:, 0:3], 0.0, sem=s_vm)

        tl = ltiles(0, LE)
        p.wait("tensor", s_vm)
        for m in range(32):
            p.guard("sync", ("wgt3", m % 2))
            wb = wgt3[:, (m % 2) * 1024:(m % 2) * 1024 + 1024]
            p.dma("sync", wb.rearrange("pp (k w) -> pp k w", k=DCH),
                  WinT[li, :, m * 128:(m + 1) * 128]
                  .rearrange("(k pp) w -> k pp w", pp=128)
                  .transpose([1, 0, 2]), s_ld)
            p.wait("tensor", s_ld)
            pse = psA if m % 2 == 0 else psB
            for i, (a, b) in enumerate(tl):
                for k in range(DCH):
                    mm(pse[:, 0:b - a], wb[:, k * 128:(k + 1) * 128],
                       xnA[k][:, a:b], k == 0, k == DCH - 1,
                       sem=(s_tm if k == DCH - 1 else None))
                p.wait("scalar", s_tm)
                if m < 16:
                    scp(convB[:, 3 + a:3 + b], pse[:, 0:b - a], sem=s_sm)
                else:
                    p.guard("scalar", ("zrowb",)) if i == 0 else None
                    act(wkB[:, a:b], pse[:, 0:b - a], AF.Silu, sem=s_sm)
                p.wait("tensor", s_sm)
            p.release(("wgt3", m % 2), s_sm)
            if m < 16:
                p.guard("vector", ("convout",))
                p.wait("vector", s_sm)
                vo("tensor_scalar_mul", wkC, convB[:, 0:LE],
                   convw_s[:, m * 4:m * 4 + 1])
                for k in range(1, 4):
                    vo("scalar_tensor_tensor", wkC, convB[:, k:k + LE],
                       convw_s[:, m * 4 + k:m * 4 + k + 1],
                       wkC, OP.mult, OP.add,
                       sem=(s_vm if k == 3 else None))
                p.wait("scalar", s_vm)
                p.guard("scalar", ("xcout",))
                act(wkD, wkC, AF.Silu, bias=convb_s[:, m:m + 1], sem=s_sm)
                p.release(("convout",), s_sm)
                p.wait("gpsimd", s_sm)
                p.dma("gpsimd", xc_hbm[m * 128:(m + 1) * 128, :], wkD, s_gd)
                p.wait("tensor", s_sm)
                for i, (a, b) in enumerate(tl):
                    mm(psC[i][0:96, 0:b - a],
                       WxT_s[:, m * 96:(m + 1) * 96], wkD[:, a:b],
                       m == 0, m == 15,
                       sem=(s_tm if (m == 15 and i == len(tl) - 1)
                            else None))
                p.release(("xcout",), s_tm, s_gd)
            else:
                p.wait("gpsimd", s_sm)
                p.dma("gpsimd", sz_hbm[(m - 16) * 128:(m - 15) * 128, :],
                      wkB[:, 0:LE], s_gd)
                p.release(("zrowb",), s_gd)
        p.wait("vector", s_tm)
        for i, (a, b) in enumerate(tl):
            vo("tensor_copy", dtb[:, a:b], psC[i][0:RK, 0:b - a])
            vo("tensor_copy", BCr[:, a:b], psC[i][RK:96, 0:b - a],
               sem=(s_vm if i == len(tl) - 1 else None))
        p.wait("scalar", s_vm)
        p.dma("scalar", bc_hbm, BCr, s_sd)
        p.wait("tensor", s_vm)
        for m in range(ECH):
            p.guard("scalar", ("dlA",))
            for i, (a, b) in enumerate(tl):
                pse = psA if (m + i) % 2 == 0 else psB
                p.guard("tensor", ("psAB2", (m + i) % 2))
                mm(pse[:, 0:b - a], WdtT_s[:, m * 128:(m + 1) * 128],
                   dtb[:, a:b], True, True, sem=s_tm)
                p.wait("scalar", s_tm)
                act(wkA[:, a:b], pse[:, 0:b - a], AF.Sigmoid,
                    scale=cst[:, 2:3], bias=nbdt[:, m:m + 1], sem=s_sm)
                p.release(("psAB2", (m + i) % 2), s_sm)
            act(wkA, wkA, AF.Ln, sem=s_sm)
            p.wait("vector", s_sm)
            p.guard("vector", ("dlB",))
            vo("tensor_mul", wkB, wkA, mask_b, sem=s_vm)
            p.release(("dlA",), s_vm)
            p.wait("gpsimd", s_vm)
            p.dma("gpsimd", dl_hbm[m * 128:(m + 1) * 128, :], wkB, s_gd)
            p.release(("dlB",), s_gd)

        p.wait("vector", s_tm)
        p.wait("vector", s_sm)
        vo("memset", onesbig, 1.0, sem=s_vm)
        for h in range(2):
            ha, hb = (0, LH) if h == 0 else (LH, LE)
            w = hb - ha
            lo = max(sst, ha) - ha
            hi_ = (min(cut, hb - 1) - ha) + 1
            p.guard("scalar", ("bbc",))
            for n in range(NST):
                dst = (big1 if n < 8 else big2)
                p.dma("scalar", dst[:, (n % 8) * LH:(n % 8) * LH + w],
                      bc_hbm[n:n + 1, ha:hb].broadcast_to([128, w]), s_sd)
            p.wait("vector", s_sd)
            p.wait("sync", s_gd)
            for g in range(ECH):
                p.guard("sync", ("gAB", g % 2))
                dl = wkA[:, (g % 2) * LH:(g % 2) * LH + w]
                xc = wkB[:, (g % 2) * LH:(g % 2) * LH + w]
                p.dma("sync", dl, dl_hbm[g * 128:(g + 1) * 128, ha:hb], s_ld)
                p.dma("sync", xc, xc_hbm[g * 128:(g + 1) * 128, ha:hb], s_ld)
                p.wait("vector", s_ld)
                ini = 0.0 if h == 0 else scol[:, g * 4 + 1:g * 4 + 2]
                p.guard("vector", ("Sg",))
                vo("tensor_tensor_scan", wkC[:, 0:w], onesbig[:, 0:w], dl,
                   ini, OP.mult, OP.add)
                p.guard("vector", ("ug",))
                vo("scalar_tensor_tensor", wkD[:, 0:w], dl, -1.0, xc,
                   OP.mult, OP.mult, sem=s_vm)
                p.release(("gAB", g % 2), s_vm)
                p.wait("scalar", s_vm)
                if h == 0:
                    scp(scol[:, g * 4:g * 4 + 1], wkC[:, sst - 1:sst])
                    scp(scol[:, g * 4 + 1:g * 4 + 2], wkC[:, w - 1:w],
                        sem=s_sm)
                    refc = scol[:, g * 4 + 1:g * 4 + 2]
                else:
                    scp(scol[:, g * 4 + 2:g * 4 + 3],
                        wkC[:, cut - ha:cut - ha + 1], sem=s_sm)
                    refc = scol[:, g * 4 + 2:g * 4 + 3]
                p.wait("vector", s_sm)
                vo("tensor_scalar_mul", tiny[:, 0:1], refc, -1.0, sem=s_vm)
                p.wait("vector", s_vm)
                vo("tensor_scalar_mul", tiny[:, 1:17],
                   A_s[:, g * NST:(g + 1) * NST], tiny[:, 0:1], sem=s_vm)
                p.wait("scalar", s_vm)
                for n in range(NST):
                    p.guard("scalar", ("Ebuf",))
                    act(wkE[:, 0:w], wkC[:, 0:w], AF.Exp,
                        scale=A_s[:, g * NST + n:g * NST + n + 1],
                        bias=tiny[:, n + 1:n + 2], sem=s_sm)
                    p.wait("gpsimd", s_sm)
                    p.wait("gpsimd", s_vm)
                    go("tensor_mul", wkE[:, 0:w], wkE[:, 0:w], wkD[:, 0:w],
                       sem=s_gm)
                    p.wait("vector", s_gm)
                    bsl = (big1 if n < 8 else big2)[:, (n % 8) * LH + lo:
                                                    (n % 8) * LH + hi_]
                    dstp = (sumB0 if h == 0 else sumB1)
                    vo("scalar_tensor_tensor", wkE[:, lo:hi_],
                       wkE[:, lo:hi_], 1.0, bsl, OP.mult, OP.mult,
                       accum_out=dstp[:, g * NST + n:g * NST + n + 1],
                       sem=s_vm)
                    p.release(("Ebuf",), s_vm)
                    p.wait("scalar", s_vm)
                p.release(("Sg",), s_vm, s_sm)
                p.release(("ug",), s_vm, s_gm)
            p.release(("bbc",), s_vm, s_gm)
        p.wait("vector", s_sm)
        for g in range(ECH):
            vo("tensor_sub", tiny[:, g:g + 1],
               scol[:, g * 4 + 1:g * 4 + 2], scol[:, g * 4 + 2:g * 4 + 3])
            vo("tensor_sub", tiny[:, 16 + g:17 + g],
               scol[:, g * 4:g * 4 + 1], scol[:, g * 4 + 2:g * 4 + 3],
               sem=(s_vm if g == ECH - 1 else None))
        p.wait("vector", s_vm)
        for g in range(ECH):
            vo("tensor_scalar_mul", htmp[:, g * NST:(g + 1) * NST],
               A_s[:, g * NST:(g + 1) * NST], tiny[:, g:g + 1])
            vo("tensor_scalar_mul",
               htmp[:, 256 + g * NST:256 + (g + 1) * NST],
               A_s[:, g * NST:(g + 1) * NST], tiny[:, 16 + g:17 + g],
               sem=(s_vm if g == ECH - 1 else None))
        p.wait("scalar", s_vm)
        act(htmp[:, 0:256], htmp[:, 0:256], AF.Exp)
        act(sumA, htmp[:, 256:512], AF.Exp, sem=s_sm)
        p.wait("vector", s_sm)
        vo("tensor_mul", sumB0, sumB0, htmp[:, 0:256], sem=s_vm)
        p.wait("vector", s_vm)
        vo("tensor_add", sumB0, sumB0, sumB1, sem=s_vm)
        p.wait("scalar", s_vm)
        p.dma("scalar", cr_ag_in[:, 0:256], sumB0, s_sd)
        p.dma("scalar", cr_ag_in[:, 256:512], sumA, s_sd)
        p.wait("gpsimd", s_sd)
        s_cc.n += 1
        p.emit("gpsimd", lambda e, h=s_cc.h: e.collective_compute(
            "AllGather", OP.bypass, replica_groups=[list(range(NCORE))],
            ins=[cr_ag_in], outs=[cr_ag_out]).then_inc(h, 1))
        p.wait("vector", s_cc)
        p.wait("sync", s_cc)
        vo("memset", h_in, 0.0, sem=s_vm)
        p.wait("vector", s_vm)
        for j in range(NCORE - 1):
            p.guard("sync", ("crt",))
            p.dma("sync", crtile, cr_ag_out[j * 128:(j + 1) * 128, :], s_ld)
            p.wait("vector", s_ld)
            vo("tensor_mul", htmp[:, 0:256], crtile[:, 256:512], h_in,
               sem=s_vm)
            p.wait("vector", s_vm)
            vo("tensor_add", htmp[:, 0:256], htmp[:, 0:256],
               crtile[:, 0:256], sem=s_vm)
            p.release(("crt",), s_vm)
            p.wait("vector", s_vm)
            vo("tensor_sub", htmp[:, 256:512], htmp[:, 0:256], h_in,
               sem=s_vm)
            p.wait("vector", s_vm)
            vo("scalar_tensor_tensor", h_in, htmp[:, 256:512],
               fold_s[:, j:j + 1], h_in, OP.mult, OP.add, sem=s_vm)
            p.wait("vector", s_vm)

        for h in range(2):
            ha, hb = (0, LH) if h == 0 else (LH, LE)
            w = hb - ha
            lo = max(sst, ha) - ha
            for ns in range(2):
                p.guard("scalar", ("bbc",))
                for n8 in range(8):
                    n = ns * 8 + n8
                    p.dma("scalar", big1[:, n8 * LH:n8 * LH + w],
                          bc_hbm[n:n + 1, ha:hb].broadcast_to([128, w]),
                          s_sd)
                    p.dma("scalar", big2[:, n8 * LH:n8 * LH + w],
                          bc_hbm[16 + n:17 + n, ha:hb].broadcast_to([128, w]),
                          s_sd)
                p.wait("vector", s_sd)
                p.wait("gpsimd", s_sd)
                p.wait("sync", s_gd)
                for g in range(ECH):
                    p.guard("sync", ("gAB", g % 2))
                    dl = wkA[:, (g % 2) * LH:(g % 2) * LH + w]
                    xc = wkB[:, (g % 2) * LH:(g % 2) * LH + w]
                    p.dma("sync", dl, dl_hbm[g * 128:(g + 1) * 128, ha:hb],
                          s_ld)
                    p.dma("sync", xc, xc_hbm[g * 128:(g + 1) * 128, ha:hb],
                          s_ld)
                    p.wait("vector", s_ld)
                    p.wait("scalar", s_ld)
                    p.guard("vector", ("ug",))
                    vo("scalar_tensor_tensor", wkD[:, 0:w], dl, -1.0, xc,
                       OP.mult, OP.mult, sem=s_vm)
                    ybuf = (big3 if g < 8 else big4)[:, (g % 8) * LH:
                                                     (g % 8) * LH + w]
                    for n8 in range(8):
                        n = ns * 8 + n8
                        pc = g * NST + n
                        p.guard("scalar", ("dAb",))
                        act(wkC[:, lo:w], dl[:, lo:w], AF.Exp,
                            scale=nA_s[:, pc:pc + 1], sem=s_sm)
                        p.guard("gpsimd", ("BXb",))
                        p.wait("gpsimd", s_vm)
                        go("tensor_mul", wkE[:, lo:w], wkD[:, lo:w],
                           big1[:, n8 * LH + lo:n8 * LH + w], sem=s_gm)
                        p.wait("vector", s_gm)
                        p.wait("vector", s_sm)
                        ini = (h_in[:, pc:pc + 1] if h == 0
                               else stt_c[:, pc:pc + 1])
                        vo("tensor_tensor_scan", wkE[:, lo:w],
                           wkC[:, lo:w], wkE[:, lo:w], ini,
                           OP.mult, OP.add, sem=s_vm)
                        if h == 0:
                            p.wait("scalar", s_vm)
                            scp(stt_c[:, pc:pc + 1], wkE[:, w - 1:w],
                                sem=s_sm)
                        if n == 0:
                            vo("scalar_tensor_tensor", ybuf[:, lo:w],
                               wkE[:, lo:w], 1.0,
                               big2[:, n8 * LH + lo:n8 * LH + w],
                               OP.mult, OP.mult, sem=s_vm)
                        else:
                            vo("scalar_tensor_tensor", wkC[:, lo:w],
                               wkE[:, lo:w], 1.0,
                               big2[:, n8 * LH + lo:n8 * LH + w],
                               OP.mult, OP.mult)
                            vo("tensor_add", ybuf[:, lo:w], ybuf[:, lo:w],
                               wkC[:, lo:w], sem=s_vm)
                        p.release(("dAb",), s_vm)
                        if h == 0:
                            p.release(("BXb",), s_vm, s_sm)
                        else:
                            p.release(("BXb",), s_vm)
                    p.release(("ug",), s_vm, s_gm)
                    p.release(("gAB", g % 2), s_vm, s_sm, s_gm)
            p.wait("sync", s_gd)
            for g in range(ECH):
                p.guard("sync", ("gAB", g % 2))
                xc = wkB[:, (g % 2) * LH:(g % 2) * LH + w]
                sz = wkA[:, (g % 2) * LH:(g % 2) * LH + w]
                p.dma("sync", xc, xc_hbm[g * 128:(g + 1) * 128, ha:hb], s_ld)
                p.dma("sync", sz, sz_hbm[g * 128:(g + 1) * 128, ha:hb], s_ld)
                ybuf = (big3 if g < 8 else big4)[:, (g % 8) * LH:
                                                 (g % 8) * LH + w]
                p.wait("vector", s_ld)
                vo("scalar_tensor_tensor", ybuf[:, lo:w], xc[:, lo:w],
                   Dp_s[:, g:g + 1], ybuf[:, lo:w], OP.mult, OP.add)
                vo("tensor_mul", ybuf[:, lo:w], ybuf[:, lo:w], sz[:, lo:w],
                   sem=s_vm)
                p.release(("gAB", g % 2), s_vm)
            if DBG_MAMBA and li == 0:
                p.wait("gpsimd", s_vm)
                p.dma("gpsimd", dbg[f"yz3h{h}"], big3, s_out)
                p.dma("gpsimd", dbg[f"yz4h{h}"], big4, s_out)
                p.wait("sync", s_out)
            p.wait("sync", s_vm)
            wtl = ltiles(ha + lo, hb)
            for dch in range(DCH):
                p.guard("sync", ("wout", dch % 2))
                wb = (wkC if dch % 2 == 0 else wkD)[:, 0:2048]
                p.dma("sync", wb.rearrange("pp (g w) -> pp g w", g=ECH),
                      WoutT[li, :, dch * 128:(dch + 1) * 128]
                      .rearrange("(g pp) w -> g pp w", pp=128)
                      .transpose([1, 0, 2]), s_ld)
                p.wait("tensor", s_ld)
                for i, (a, b) in enumerate(wtl):
                    ps = psA if (dch + i) % 2 == 0 else psB
                    p.guard("tensor", ("psAB3", (dch + i) % 2))
                    for g in range(ECH):
                        ybuf = (big3 if g < 8 else big4)
                        mm(ps[:, 0:b - a], wb[:, g * 128:(g + 1) * 128],
                           ybuf[:, (g % 8) * LH + (a - ha):
                                (g % 8) * LH + (b - ha)],
                           g == 0, g == ECH - 1,
                           sem=(s_tm if g == ECH - 1 else None))
                    p.guard("sync", ("xarow",))
                    xb = wkE[:, 0:b - a]
                    p.dma("sync", xb, xa_hbm[dch * 128:(dch + 1) * 128, a:b],
                          s_ld)
                    p.wait("vector", s_ld)
                    p.wait("vector", s_tm)
                    vo("tensor_add", xb, xb, ps[:, 0:b - a], sem=s_vm)
                    p.release(("psAB3", (dch + i) % 2), s_vm)
                    p.wait("gpsimd", s_vm)
                    p.dma("gpsimd", xa_hbm[dch * 128:(dch + 1) * 128, a:b],
                          xb, s_gd)
                    p.release(("xarow",), s_gd)
                p.release(("wout", dch % 2), s_tm)
            barrier()

    zero_acc()
    ffn_sublayer(first=True, last=(stage == 1 and not tap_aps and ffn_cut >= 99))
    if "x1" in tap_aps:
        barrier()
        p.dma("gpsimd", tap_aps["x1"], x_hbm, s_out)
    if stage >= 2:
        ln_stats_apply(x_hbm)
        p.wait("gpsimd", s_vm)
        for c in range(DCH):
            p.dma("gpsimd", xa_hbm[c * 128:(c + 1) * 128, :], xnA[c], s_gd)
            p.release(("xnA", c), s_gd)
        barrier()
        if "xa0" in tap_aps:
            p.dma("gpsimd", tap_aps["xa0"], xa_hbm, s_out)
            barrier()
        mamba_layer(0)
        barrier()
        if "xa1" in tap_aps:
            p.dma("gpsimd", tap_aps["xa1"], xa_hbm, s_out)
            barrier()
        if nmamba > 1:
            mamba_layer(1)
        barrier()
        if "xa2" in tap_aps:
            p.dma("gpsimd", tap_aps["xa2"], xa_hbm, s_out)
            barrier()
    if stage >= 3:
        zero_acc()
        ffn_sublayer(first=False, last=True)
    elif stage == 2 or (stage == 1 and (tap_aps or ffn_cut < 99)):
        barrier()
        for c in range(DCH):
            p.guard("sync", ("fin",))
            p.dma("sync", wkA, x_hbm[c * 128:(c + 1) * 128, :], s_ld)
            if stage >= 2:
                p.dma("sync", wkB, xa_hbm[c * 128:(c + 1) * 128, :], s_ld)
            p.wait("vector", s_ld)
            if stage >= 2:
                vo("tensor_add", wkA, wkA, wkB, sem=s_vm)
            else:
                vo("tensor_scalar_mul", wkA, wkA, 1.0, sem=s_vm)
            p.wait("gpsimd", s_vm)
            p.dma("gpsimd", out_x[c * 128:(c + 1) * 128, :], wkA[:, HALO:LE],
                  s_out)
            p.release(("fin",), s_out)

    barrier()
    p.wait("gpsimd", s_out)
    p.wait("sync", s_out)

    with nc.Block() as block:
        @block.sync
        def _(sync):
            for fn in p.ops["sync"]:
                fn(sync)

        @block.tensor
        def _(tensor):
            for fn in p.ops["tensor"]:
                fn(tensor)

        @block.vector
        def _(vector):
            for fn in p.ops["vector"]:
                fn(vector)

        @block.scalar
        def _(scalar):
            for fn in p.ops["scalar"]:
                fn(scalar)

        @block.gpsimd
        def _(gpsimd):
            for fn in p.ops["gpsimd"]:
                fn(gpsimd)

    return nc


def _prep(inputs):
    f = np.float32
    x = np.ascontiguousarray(np.asarray(inputs["L0_attn_x"], f)[0])
    Wg = np.asarray(inputs["L0_ffn_Wg"], f)
    W1 = np.asarray(inputs["L0_ffn_W1"], f)
    W2 = np.asarray(inputs["L0_ffn_W2"], f)
    W3 = np.asarray(inputs["L0_ffn_W3"], f)
    norm_w = np.asarray(inputs["L0_attn_norm_w"], f)
    Win = np.asarray(inputs["L0_attn_Win"], f)
    conv_w = np.asarray(inputs["L0_attn_conv_w"], f)
    conv_b = np.asarray(inputs["L0_attn_conv_b"], f)
    Wx = np.asarray(inputs["L0_attn_Wx"], f)
    Wdt = np.asarray(inputs["L0_attn_Wdt"], f)
    bdt = np.asarray(inputs["L0_attn_bdt"], f)
    A_log = np.asarray(inputs["L0_attn_A_log"], f)
    Dp = np.asarray(inputs["L0_attn_Dp"], f)
    Wout = np.asarray(inputs["L0_attn_Wout"], f)

    com = {
        "bd16": np.kron(np.eye(8, dtype=f), np.ones((16, 16), f)),
        "sel16": np.kron(np.eye(8, dtype=f), np.ones((16, 1), f))
        * (np.arange(128)[:, None] % 16 == 0),
        "ones_r": np.ones((1, 128), f),
        "ones_c": np.ones((128, 1), f),
        "qiota": (np.arange(128, dtype=f)[:, None]
                  + 128.0 * np.arange(QCH, dtype=f)[None, :]).astype(f),
        "idmat": np.eye(128, dtype=f),
        "WgT": np.ascontiguousarray(
            Wg.T.reshape(DCH, 128, EFF).transpose(1, 0, 2)
            .reshape(128, DCH * EFF)),
        "W1T": np.ascontiguousarray(W1.transpose(0, 2, 1)),
        "W3T": np.ascontiguousarray(W3.transpose(0, 2, 1)),
        "W2T": np.ascontiguousarray(W2.transpose(0, 2, 1)),
        "nrmw": np.ascontiguousarray(
            norm_w.reshape(2, DCH, 128).transpose(0, 2, 1)),
        "WinT": np.ascontiguousarray(Win.transpose(0, 2, 1)),
        "convw": np.ascontiguousarray(
            conv_w.reshape(2, ECH, 128, 4).transpose(0, 2, 1, 3)
            .reshape(2, 128, ECH * 4)),
        "convb": np.ascontiguousarray(
            conv_b.reshape(2, ECH, 128).transpose(0, 2, 1)),
        "WxT": np.ascontiguousarray(Wx.transpose(0, 2, 1)),
        "WdtT": np.ascontiguousarray(Wdt.transpose(0, 2, 1)),
        "bdt_c": np.ascontiguousarray(
            bdt.reshape(2, ECH, 128).transpose(0, 2, 1)),
        "Alog_c": np.ascontiguousarray(
            A_log.reshape(2, ECH, 128, NST).transpose(0, 2, 1, 3)
            .reshape(2, 128, ECH * NST)),
        "Dp_c": np.ascontiguousarray(
            Dp.reshape(2, ECH, 128).transpose(0, 2, 1)),
        "WoutT": np.ascontiguousarray(Wout.transpose(0, 2, 1)),
    }
    in_maps = []
    for c in range(NCORE):
        if c == 0:
            sh = np.concatenate([np.zeros((HALO, D), f), x[:L_OWN]], 0)
            mk = np.concatenate([np.zeros(HALO, f), np.ones(L_OWN, f)])
        else:
            sh = x[c * L_OWN - HALO:(c + 1) * L_OWN]
            mk = np.ones(LE, f)
        m = dict(com)
        m["x_in"] = np.ascontiguousarray(sh.T)
        m["maskrow"] = np.ascontiguousarray(mk[None, :])
        fm = np.zeros((128, 8), f)
        fm[:, :c] = 1.0
        m["foldmask"] = fm
        in_maps.append(m)
    return in_maps


def kernel(**inputs):
    in_maps = _prep(inputs)
    nc = build_program(stage=3)
    res = run_bass_kernel_spmd(nc, in_maps, core_ids=list(range(NCORE)))
    outs = [np.asarray(r["out_x"]) for r in res.results]
    full = np.concatenate([o.T for o in outs], axis=0)
    return np.ascontiguousarray(full[None, :, :]).astype(np.float32)


# version 03
# speedup vs baseline: 1.0044x; 1.0044x over previous
import numpy as np
import concourse.bass as bass
import concourse.mybir as mybir
from concourse.bass_utils import run_bass_kernel_spmd
from concourse.bass_core_rust import IndirectOffsetOnAxis

F32 = mybir.dt.float32
I32 = mybir.dt.int32
AF = mybir.ActivationFunctionType
OP = mybir.AluOpType

NCORE = 8
L_OWN = 2048
HALO = 6
LE = L_OWN + HALO
LE1 = LE + 1
D = 1024
DCH = 8
EFF = 8
HFF = 2048
HCH = 16
CAP = 2048
CPAD = 384
QCH = 3
ED = 2048
ECH = 16
NST = 16
RK = 64
EPS = 1e-5
BIS = 45
SBND = 64.0
NT = 512
LH = 1027
ENGS = ("sync", "tensor", "vector", "scalar", "gpsimd")


def ltiles(lo, hi, step=NT):
    out = []
    c = lo
    while c < hi:
        out.append((c, min(c + step, hi)))
        c = min(c + step, hi)
    return out


class Sem:
    __slots__ = ("h", "n")

    def __init__(self, h):
        self.h = h
        self.n = 0


class PB:
    def __init__(self, nc):
        self.nc = nc
        self.ops = {e: [] for e in ENGS}
        self.gates = {}

    def sem(self, name):
        return Sem(self.nc.semaphore(name).__enter__())

    def emit(self, eng, fn):
        self.ops[eng].append(fn)

    def wait(self, eng, sem, val=None):
        v = sem.n if val is None else val
        if v <= 0:
            return
        self.emit(eng, lambda e, h=sem.h, vv=v: e.wait_ge(h, vv))

    def dma(self, q, out, in_, sem, **kw):
        sem.n += 16
        self.emit(q, lambda e, o=out, i=in_, h=sem.h, k=dict(kw):
                  e.dma_start(out=o, in_=i, **k).then_inc(h, 16))

    def guard(self, eng, key):
        for (sem, cnt) in self.gates.get(key, ()):
            self.wait(eng, sem, cnt)

    def release(self, key, *sems):
        self.gates[key] = tuple((s, s.n) for s in sems)


DBG_FFN = False
DBG_MAMBA = False


def build_program(stage=3, taps=(), nmamba=2, ffn_cut=99):
    nc = bass.Bass("TRN2", target_bir_lowering=False,
                   detect_race_conditions=False)
    taps = list(taps)

    def inp(name, shape, dtype=F32):
        return nc.dram_tensor(name, shape, dtype, kind="ExternalInput").ap()

    x_in = inp("x_in", [D, LE])
    maskrow = inp("maskrow", [1, LE])
    foldmask = inp("foldmask", [128, 8])
    bd16 = inp("bd16", [128, 128])
    sel16 = inp("sel16", [128, EFF])
    ones_r = inp("ones_r", [1, 128])
    ones_c = inp("ones_c", [128, 1])
    qiota = inp("qiota", [128, QCH])
    idmat = inp("idmat", [128, 128])
    WgT = inp("WgT", [128, DCH * EFF])
    W1T = inp("W1T", [EFF, D, HFF])
    W3T = inp("W3T", [EFF, D, HFF])
    W2T = inp("W2T", [EFF, HFF, D])
    nrmw = inp("nrmw", [2, 128, DCH])
    WinT = inp("WinT", [2, D, 2 * ED])
    convw = inp("convw", [2, 128, ECH * 4])
    convb = inp("convb", [2, 128, ECH])
    WxT = inp("WxT", [2, ED, 96])
    WdtT = inp("WdtT", [2, RK, ED])
    bdt_c = inp("bdt_c", [2, 128, ECH])
    Alog_c = inp("Alog_c", [2, 128, ECH * NST])
    Dp_c = inp("Dp_c", [2, 128, ECH])
    WoutT = inp("WoutT", [2, ED, D])

    out_x = nc.dram_tensor("out_x", [D, L_OWN], F32, kind="ExternalOutput").ap()
    dbg = {}
    if DBG_MAMBA:
        for nm in ("yz3h0", "yz4h0", "yz3h1", "yz4h1"):
            dbg[nm] = nc.dram_tensor("dbg_" + nm, [128, 8 * LH], F32,
                                     kind="ExternalOutput").ap()
    if DBG_FFN:
        for nm, shp in (("G", [128, 1024]), ("bc0", [128, 16]),
                        ("idxf", [128, 24]), ("wsl", [128, 192]),
                        ("wh", [LE1, EFF]), ("xntm", [LE1, D]),
                        ("cum8", [8, LE]), ("sloc", [8, LE1]),
                        ("bcol", [128, 16]),
                        ("xgtm", [128, 3072]), ("xgdm", [128, 3072]),
                        ("h1s", [128, 6144]), ("hp", [128, 6144]),
                        ("ydm", [128, 3072]), ("ytm", [128, 3072])):
            dbg[nm] = nc.dram_tensor("dbg_" + nm, shp, F32,
                                     kind="ExternalOutput").ap()
    tap_aps = {}
    for t in taps:
        tap_aps[t] = nc.dram_tensor("tap_" + t, [D, LE], F32,
                                    kind="ExternalOutput").ap()

    def scr(name, shape, dtype=F32):
        return nc.dram_tensor(name, shape, dtype, kind="Internal").ap()

    x_hbm = scr("x_hbm", [D, LE])
    xa_hbm = scr("xa_hbm", [D, LE])
    xn_tm = scr("xn_tm", [LE1, D])
    acc_tm = scr("acc_tm", [LE1, D])
    w_hbm = scr("w_hbm", [LE1, EFF])
    sc_ag_in = scr("sc_ag_in", [EFF, L_OWN])
    sc_ag_out = nc.dram_tensor("sc_ag_out", [NCORE * EFF, L_OWN], F32,
                               kind="Internal", addr_space="Shared").ap()
    xc_hbm = scr("xc_hbm", [ED, LE])
    dl_hbm = scr("dl_hbm", [ED, LE])
    sz_hbm = scr("sz_hbm", [ED, LE])
    bc_hbm = scr("bc_hbm", [32, LE])
    cr_ag_in = scr("cr_ag_in", [128, 512])
    cr_ag_out = nc.dram_tensor("cr_ag_out", [NCORE * 128, 512], F32,
                               kind="Internal", addr_space="Shared").ap()

    def sb(name, shape, dtype=F32):
        return nc.alloc_sbuf_tensor(name, shape, dtype).ap()

    PH1 = sb("PH1", [128, 8 * LE])
    PH2 = sb("PH2", [128, 8 * LE])
    xnA = [PH1[:, c * LE:(c + 1) * LE] for c in range(DCH)]
    big3 = PH1[:, 0:8 * LH]
    big4 = PH1[:, 8 * LH:16 * LH]
    big1 = PH2[:, 0:8 * LH]
    big2 = PH2[:, 8 * LH:16 * LH]
    xg_tm = big3[:, 0:QCH * 1024]
    xg_dm = big3[:, QCH * 1024:QCH * 1024 + DCH * CPAD]
    y_dm = big4[:, 0:DCH * CPAD]
    y_tm = big4[:, DCH * CPAD:DCH * CPAD + QCH * 1024]
    h1s = big1[:, 0:HCH * CPAD]
    hp = big2[:, 0:HCH * CPAD]
    wkA = sb("wkA", [128, LE])
    wkB = sb("wkB", [128, LE])
    wkC = sb("wkC", [128, LE])
    wkD = sb("wkD", [128, LE])
    wkE = sb("wkE", [128, LE])
    m_b = wkC
    r_b = wkD
    gj = wkC[:, 0:1024]
    cnt_b = wkD
    WdtT_s = PH2[0:RK, 0:ED]
    WxT_s = PH2[:, ED:ED + ECH * 96]
    srow = PH2[0:1, 3584:3584 + LE]
    rrow = PH2[0:1, 5638:5638 + LE]
    G = PH2[:, 7168:7168 + 1024]
    mask_b = PH2[:, 8216:8216 + LE]
    convB = PH2[:, 10270:10270 + LE + 4]
    wgt3 = PH2[:, 14382:14382 + 2 * 1024]
    wgt1 = PH2[:, 6144:6144 + 2 * NT]
    onesbig = PH1[:, 0:LH]
    onesLE = PH1[:, 0:LE]
    scanLE = PH1[:, LE:2 * LE]
    maskLE = PH1[:, 2 * LE:3 * LE]
    zrow = sb("zrow", [1, D])
    bd16_s = sb("bd16_s", [128, 128])
    sel16_s = sb("sel16_s", [128, EFF])
    ones_r_s = sb("ones_r_s", [1, 128])
    ones_c_s = sb("ones_c_s", [128, 1])
    qiota_s = sb("qiota_s", [128, QCH])
    idm = sb("idm", [128, 128])
    fold_s = sb("fold_s", [128, 8])
    WgT_s = sb("WgT_s", [128, DCH * EFF])
    mrow = sb("mrow", [1, LE])
    small8 = sb("small8", [8, LE1])
    sloc = small8
    w8 = small8
    cum8 = PH2[0:8, 8216:8216 + LE]
    wmask = PH2[0:8, 10270:10270 + LE]
    small2 = sb("small2", [128, LE])
    dtb = small2[0:RK, :]
    BCr = small2[64:96, :]
    bcol = sb("bcol", [128, 16])
    cst = sb("cst", [128, 4])
    idxf = sb("idxf", [128, EFF * QCH])
    idxi = sb("idxi", [128, EFF * QCH], I32)
    wslot = sb("wslot", [128, EFF * QCH * 8])
    t8 = sb("t8", [EFF, 2])
    A_s = sb("A_s", [128, ECH * NST])
    nA_s = sb("nA_s", [128, ECH * NST])
    sumB0 = PH1[:, 2 * LH:2 * LH + 256]
    sumB1 = PH1[:, 2 * LH + 256:2 * LH + 512]
    sumA = PH1[:, 2 * LH + 512:2 * LH + 768]
    crtile = PH1[:, 2 * LH + 768:2 * LH + 1280]
    htmp = PH1[:, 2 * LH + 1280:2 * LH + 1792]
    h_in = sb("h_in", [128, 256])
    scol = sb("scol", [128, 64])
    stt_c = sb("stt_c", [128, 256])
    nbdt = sb("nbdt", [128, ECH])
    tiny = sb("tiny", [128, 64])
    nrmw_s = sb("nrmw_s", [128, DCH])
    convw_s = sb("convw_s", [128, ECH * 4])
    convb_s = sb("convb_s", [128, ECH])
    Dp_s = sb("Dp_s", [128, ECH])
    Alog_s = sb("Alog_s", [128, ECH * NST])
    bdt_s = sb("bdt_s", [128, ECH])

    psA = nc.alloc_psum_tensor("psA", [128, NT], F32).ap()
    psB = nc.alloc_psum_tensor("psB", [128, NT], F32).ap()
    psT = nc.alloc_psum_tensor("psT", [128, NT], F32).ap()
    psC = [nc.alloc_psum_tensor(f"psC{i}", [128, NT], F32).ap()
           for i in range(5)]
    psR = [psT, psA, psB, psC[4]]

    p = PB(nc)
    s_ld = p.sem("s_ld")
    s_gd = p.sem("s_gd")
    s_sd = p.sem("s_sd")
    s_tm = p.sem("s_tm")
    s_vm = p.sem("s_vm")
    s_sm = p.sem("s_sm")
    s_gm = p.sem("s_gm")
    s_cc = p.sem("s_cc")
    s_out = p.sem("s_out")
    ALLS = (s_ld, s_gd, s_sd, s_tm, s_vm, s_sm, s_gm, s_cc)

    def mm(ps, lhsT, rhs, start, stop, sem=None):
        if sem is not None:
            sem.n += 1
            p.emit("tensor", lambda e, o=ps, l=lhsT, r=rhs, a=start, b=stop,
                   h=sem.h: e.matmul(o, l, r, start=a, stop=b).then_inc(h, 1))
        else:
            p.emit("tensor", lambda e, o=ps, l=lhsT, r=rhs, a=start, b=stop:
                   e.matmul(o, l, r, start=a, stop=b))

    def tr(ps, in_, sem=None, base=0):
        np_ = in_.shape[0]
        ident = idm[base:base + np_, base:base + np_]
        if sem is not None:
            sem.n += 1
            p.emit("tensor", lambda e, o=ps, i=in_, m=ident, h=sem.h:
                   e.transpose(o, i, m).then_inc(h, 1))
        else:
            p.emit("tensor", lambda e, o=ps, i=in_, m=ident:
                   e.transpose(o, i, m))

    def vo(name, *args, sem=None, eng="vector", **kw):
        if sem is not None:
            sem.n += 1
            p.emit(eng, lambda e, a=args, k=dict(kw), h=sem.h:
                   getattr(e, name)(*a, **k).then_inc(h, 1))
        else:
            p.emit(eng, lambda e, a=args, k=dict(kw):
                   getattr(e, name)(*a, **k))

    def go(name, *args, sem=None, **kw):
        vo(name, *args, sem=sem, eng="gpsimd", **kw)

    def act(out, in_, func, bias=0.0, scale=1.0, accum_out=None, sem=None):
        if sem is not None:
            sem.n += 1
            p.emit("scalar", lambda e, o=out, i=in_, f=func, b=bias, s=scale,
                   ac=accum_out, h=sem.h:
                   e.activation(o, i, f, bias=b, scale=s, accum_out=ac)
                   .then_inc(h, 1))
        else:
            p.emit("scalar", lambda e, o=out, i=in_, f=func, b=bias, s=scale,
                   ac=accum_out:
                   e.activation(o, i, f, bias=b, scale=s, accum_out=ac))

    def scp(out, in_, sem=None):
        if sem is not None:
            sem.n += 1
            p.emit("scalar", lambda e, o=out, i=in_, h=sem.h:
                   e.copy(o, i).then_inc(h, 1))
        else:
            p.emit("scalar", lambda e, o=out, i=in_: e.copy(o, i))

    def ind_dma(out, out_off, in_, in_off, add=False):
        s_gd.n += 16
        cop = OP.add if add else OP.bypass
        p.emit("gpsimd", lambda e, o=out, oo=out_off, i=in_, io=in_off,
               c=cop, h=s_gd.h: e.indirect_dma_start(
                   out=o, out_offset=oo, in_=i, in_offset=io,
                   compute_op=c).then_inc(h, 16))

    def barrier():
        for e in ENGS:
            for s in ALLS:
                p.wait(e, s)

    p.dma("sync", mrow, maskrow, s_ld)
    p.dma("sync", bd16_s, bd16, s_ld)
    p.dma("sync", sel16_s, sel16, s_ld)
    p.dma("sync", ones_r_s, ones_r, s_ld)
    p.dma("sync", ones_c_s, ones_c, s_ld)
    p.dma("sync", qiota_s, qiota, s_ld)
    p.dma("sync", idm, idmat, s_ld)
    p.dma("sync", fold_s, foldmask, s_ld)
    p.dma("sync", WgT_s, WgT, s_ld)
    p.wait("vector", s_ld)
    vo("memset", cst[:, 0:1], EPS)
    vo("memset", cst[:, 1:2], -0.5)
    vo("memset", cst[:, 2:3], -1.0)
    vo("memset", cst[:, 3:4], 0.0)
    vo("memset", onesbig, 1.0)
    vo("memset", zrow, 0.0)
    vo("memset", sloc[:, LE:LE1], 0.0, sem=s_vm)
    p.wait("gpsimd", s_vm)
    p.dma("gpsimd", xn_tm[LE:LE1, :], zrow, s_gd)
    p.dma("gpsimd", w_hbm[LE:LE1, :], zrow[0:1, 0:EFF], s_gd)
    p.dma("gpsimd", x_hbm, x_in, s_gd)

    def ln_stats_apply(src_hbm, extra_hbm=None, rms_li=None):
        p.wait("sync", s_gd)
        for c in range(DCH):
            p.guard("sync", ("xnA", c))
            p.dma("sync", xnA[c], src_hbm[c * 128:(c + 1) * 128, :], s_ld)
            if extra_hbm is not None:
                p.guard("sync", ("wkA",))
                p.dma("sync", wkA, extra_hbm[c * 128:(c + 1) * 128, :], s_ld)
                p.wait("vector", s_ld)
                vo("tensor_add", xnA[c], xnA[c], wkA, sem=s_vm)
                p.release(("wkA",), s_vm)
                p.wait("gpsimd", s_vm)
                p.dma("gpsimd", x_hbm[c * 128:(c + 1) * 128, :], xnA[c], s_gd)
        p.wait("vector", s_ld)
        p.wait("scalar", s_ld)
        if extra_hbm is not None:
            p.wait("scalar", s_vm)
            p.wait("vector", s_vm)
        for c in range(DCH):
            p.guard("scalar", ("wkB",))
            act(wkB, xnA[c], AF.Square, sem=s_sm)
            if rms_li is None:
                if c == 0:
                    vo("tensor_copy", wkC, xnA[0], sem=s_vm)
                else:
                    vo("tensor_add", wkC, wkC, xnA[c], sem=s_vm)
            p.wait("vector", s_sm)
            if c == 0:
                vo("tensor_copy", wkD, wkB, sem=s_vm)
            else:
                vo("tensor_add", wkD, wkD, wkB, sem=s_vm)
            p.release(("wkB",), s_vm)
        p.wait("tensor", s_vm)
        tl = ltiles(0, LE)
        p.guard("tensor", ("psC_ln",))
        if rms_li is None:
            for i, (a, b) in enumerate(tl):
                mm(psC[i][0:1, 0:b - a], ones_c_s, wkC[:, a:b], True, True,
                   sem=(s_tm if i == len(tl) - 1 else None))
            p.wait("vector", s_tm)
            for i, (a, b) in enumerate(tl):
                vo("tensor_scalar_mul", srow[0:1, a:b], psC[i][0:1, 0:b - a],
                   1.0 / D, sem=(s_vm if i == len(tl) - 1 else None))
            p.wait("tensor", s_vm)
        for i, (a, b) in enumerate(tl):
            mm(psC[i][0:1, 0:b - a], ones_c_s, wkD[:, a:b], True, True,
               sem=(s_tm if i == len(tl) - 1 else None))
        p.wait("vector", s_tm)
        for i, (a, b) in enumerate(tl):
            vo("tensor_scalar_mul", rrow[0:1, a:b], psC[i][0:1, 0:b - a],
               1.0 / D, sem=(s_vm if i == len(tl) - 1 else None))
        if rms_li is None:
            vo("tensor_mul", wkE[0:1, 0:LE], srow, srow)
            vo("tensor_sub", rrow, rrow, wkE[0:1, 0:LE], sem=s_vm)
        p.wait("scalar", s_vm)
        act(rrow, rrow, AF.Ln, bias=cst[0:1, 0:1])
        act(rrow, rrow, AF.Exp, scale=cst[0:1, 1:2], sem=s_sm)
        p.wait("vector", s_sm)
        vo("tensor_mul", rrow, rrow, mrow, sem=s_vm)
        p.wait("tensor", s_vm)
        if rms_li is None:
            for i, (a, b) in enumerate(tl):
                mm(psC[i][:, 0:b - a], ones_r_s, srow[0:1, a:b], True, True,
                   sem=(s_tm if i == len(tl) - 1 else None))
            p.wait("vector", s_tm)
            for i, (a, b) in enumerate(tl):
                vo("tensor_copy", m_b[:, a:b], psC[i][:, 0:b - a],
                   sem=(s_vm if i == len(tl) - 1 else None))
            p.wait("tensor", s_vm)
        for i, (a, b) in enumerate(tl):
            mm(psC[i][:, 0:b - a], ones_r_s, rrow[0:1, a:b], True, True,
               sem=(s_tm if i == len(tl) - 1 else None))
        p.wait("vector", s_tm)
        for i, (a, b) in enumerate(tl):
            vo("tensor_copy", r_b[:, a:b], psC[i][:, 0:b - a],
               sem=(s_vm if i == len(tl) - 1 else None))
        p.release(("psC_ln",), s_vm)
        p.wait("tensor", s_vm)
        for c in range(DCH):
            if rms_li is None:
                vo("tensor_sub", xnA[c], xnA[c], m_b)
                vo("tensor_mul", xnA[c], xnA[c], r_b, sem=s_vm)
            else:
                vo("scalar_tensor_tensor", xnA[c], xnA[c],
                   nrmw_s[:, c:c + 1], r_b, OP.mult, OP.mult, sem=s_vm)
        p.release(("m_b r_b",), s_vm)

    def zero_acc():
        zbuf = PH1[:, 3 * LH:3 * LH + 1024]
        p.wait("vector", s_tm)
        p.wait("vector", s_sm)
        vo("memset", zbuf, 0.0, sem=s_vm)
        p.guard("gpsimd", ("acc",))
        p.wait("gpsimd", s_vm)
        for lt in range(17):
            a = lt * 128
            w = min(128, LE1 - a)
            p.dma("gpsimd", acc_tm[a:a + w, :], zbuf[0:w, :], s_gd)
        p.release(("zbuf_used",), s_gd)

    def ffn_sublayer(first, last):
        cut_out = [False]
        ln_stats_apply(x_hbm, extra_hbm=(None if first else xa_hbm))
        tl = ltiles(0, LE)

        p.wait("tensor", s_vm)
        for c in range(DCH):
            for i, (a, b) in enumerate(tl):
                mm(psC[i][0:EFF, 0:b - a], WgT_s[:, c * EFF:(c + 1) * EFF],
                   xnA[c][:, a:b], c == 0, c == DCH - 1,
                   sem=(s_tm if (c == DCH - 1 and i == len(tl) - 1) else None))
        p.wait("vector", s_tm)
        for i, (a, b) in enumerate(tl):
            vo("tensor_copy", sloc[:, a:b], psC[i][0:EFF, 0:b - a],
               sem=(s_vm if i == len(tl) - 1 else None))
        if ffn_cut <= 10:
            return
        p.wait("scalar", s_vm)
        p.dma("scalar", sc_ag_in, sloc[:, HALO:LE], s_sd)
        p.wait("gpsimd", s_sd)
        s_cc.n += 1
        p.emit("gpsimd", lambda e, h=s_cc.h: e.collective_compute(
            "AllGather", OP.bypass, replica_groups=[list(range(NCORE))],
            ins=[sc_ag_in], outs=[sc_ag_out]).then_inc(h, 1))
        p.wait("gpsimd", s_cc)
        gsrc = (sc_ag_out.rearrange("(r e) (hh w) -> r e hh w", e=EFF, hh=2)
                .transpose([1, 0, 2, 3]))
        for ee in range(EFF):
            p.dma("gpsimd", G[ee * 16:(ee + 1) * 16, :], gsrc[ee], s_gd)

        if ffn_cut <= 0:
            return
        if ffn_cut <= 11:
            return
        p.wait("tensor", s_vm)
        for lt in range(16 if ffn_cut == 12 else 17):
            a = lt * 128
            w = min(128, LE - a)
            p.guard("vector", ("tmrow",))
            for c in range(DCH):
                slot = c % 4
                p.guard("tensor", ("psT", slot))
                tr(psR[slot][0:w, 0:128], xnA[c][:, a:a + w], sem=s_tm)
                p.wait("vector", s_tm)
                vo("tensor_copy", wkA[0:w, c * 128:(c + 1) * 128],
                   psR[slot][0:w, 0:128], sem=s_vm)
                p.release(("psT", slot), s_vm)
            p.wait("sync", s_vm)
            p.dma("sync", xn_tm[a:a + w, :], wkA[0:w, 0:D], s_ld)
            p.release(("tmrow",), s_ld)
            p.release(("wkA",), s_ld)

        if ffn_cut <= 12:
            return
        p.wait("vector", s_gd)
        p.guard("vector", ("m_b r_b",))
        if DBG_FFN:
            p.wait("gpsimd", s_gd)
            p.dma("gpsimd", dbg["G"], G, s_out)
        p.wait("vector", s_tm)
        vo("memset", onesLE, 1.0)
        vo("memset", bcol[:, 0:1], -SBND)
        vo("memset", bcol[:, 1:2], SBND)
        vo("memset", bcol[:, 2:3], 0.0, sem=s_vm)
        p.wait("tensor", s_vm)
        for it in range(BIS):
            p.wait("vector", s_vm)
            vo("tensor_scalar", gj, G, bcol[:, 2:3], None, OP.is_gt)
            vo("tensor_tensor_scan", scanLE[:, 0:1024], onesLE[:, 0:1024],
               gj, cst[:, 3:4], OP.mult, OP.add, sem=s_vm)
            p.wait("tensor", s_vm)
            mm(psT[:, 0:1], bd16_s, scanLE[:, 1023:1024], True, True,
               sem=s_tm)
            vo("tensor_sub", bcol[:, 4:5], bcol[:, 2:3], bcol[:, 0:1],
               sem=s_vm)
            vo("tensor_sub", bcol[:, 9:10], bcol[:, 2:3], bcol[:, 1:2],
               sem=s_vm)
            p.wait("vector", s_tm)
            vo("tensor_scalar", bcol[:, 3:4], psT[:, 0:1], float(CAP), None,
               OP.is_gt, sem=s_vm)
            vo("tensor_scalar", bcol[:, 5:6], psT[:, 0:1], float(CAP), None,
               OP.is_le, sem=s_vm)
            p.wait("vector", s_vm)
            vo("scalar_tensor_tensor", bcol[:, 0:1], bcol[:, 4:5],
               bcol[:, 3:4], bcol[:, 0:1], OP.mult, OP.add, sem=s_vm)
            vo("scalar_tensor_tensor", bcol[:, 1:2], bcol[:, 9:10],
               bcol[:, 5:6], bcol[:, 1:2], OP.mult, OP.add, sem=s_vm)
            p.wait("vector", s_vm)
            vo("tensor_add", bcol[:, 2:3], bcol[:, 0:1], bcol[:, 1:2],
               sem=s_vm)
            p.wait("vector", s_vm)
            vo("tensor_scalar_mul", bcol[:, 2:3], bcol[:, 2:3], 0.5,
               sem=s_vm)
            if DBG_FFN and it == 0:
                p.wait("gpsimd", s_vm)
                p.dma("gpsimd", dbg["bc0"], bcol, s_out)
        p.wait("vector", s_vm)
        vo("tensor_copy", bcol[:, 2:3], bcol[:, 1:2], sem=s_vm)
        vo("tensor_scalar_mul", bcol[:, 6:7], bcol[:, 1:2], -1.0, sem=s_vm)
        p.wait("vector", s_vm)
        p.wait("scalar", s_vm)
        act(gj, G, AF.Exp, bias=bcol[:, 6:7], sem=s_sm)
        p.wait("vector", s_sm)
        vo("tensor_scalar", cnt_b[:, 0:1024], G, bcol[:, 2:3], None,
           OP.is_gt)
        vo("tensor_mul", gj, gj, cnt_b[:, 0:1024])
        vo("tensor_tensor_scan", scanLE[:, 0:1024], onesLE[:, 0:1024],
           gj, cst[:, 3:4], OP.mult, OP.add, sem=s_vm)
        p.wait("tensor", s_vm)
        mm(psT[:, 0:1], bd16_s, scanLE[:, 1023:1024], True, True, sem=s_tm)
        p.wait("vector", s_tm)
        vo("tensor_copy", bcol[:, 7:8], psT[:, 0:1], sem=s_vm)
        p.wait("vector", s_vm)
        vo("reciprocal", bcol[:, 8:9], bcol[:, 7:8], sem=s_vm)
        p.wait("tensor", s_vm)
        mm(psT[0:EFF, 0:3], sel16_s, bcol[:, 6:9], True, True, sem=s_tm)
        p.wait("vector", s_tm)
        vo("tensor_copy", t8[:, 0:1], psT[0:EFF, 0:1])
        vo("tensor_copy", t8[:, 1:2], psT[0:EFF, 2:3], sem=s_vm)
        p.release(("psT", 0), s_vm)
        p.wait("vector", s_vm)
        p.wait("vector", s_ld)
        vo("memset", onesbig[0:EFF, :], 1.0)
        vo("tensor_scalar_add", sloc[:, 0:LE], sloc[:, 0:LE], t8[:, 0:1],
           sem=s_vm)
        vo("tensor_scalar", wmask, sloc[:, 0:LE], 0.0, None, OP.is_gt,
           sem=s_vm)
        vo("tensor_tensor_scan", cum8[:, 0:LH], onesbig[0:EFF, 0:LH],
           wmask[:, 0:LH], 0.0, OP.mult, OP.add, sem=s_vm)
        p.wait("vector", s_vm)
        vo("tensor_tensor_scan", cum8[:, LH:LE], onesbig[0:EFF, 0:LH],
           wmask[:, LH:LE], cum8[:, LH - 1:LH], OP.mult, OP.add, sem=s_vm)
        p.wait("scalar", s_vm, s_vm.n - 2)
        act(w8[:, 0:LE], sloc[:, 0:LE], AF.Exp, sem=s_sm)
        p.wait("vector", s_sm)
        vo("tensor_scalar_mul", w8[:, 0:LE], w8[:, 0:LE], t8[:, 1:2])
        vo("tensor_scalar_mul", w8[:, 0:LE], w8[:, 0:LE], 0.5)
        vo("tensor_mul", w8[:, 0:LE], w8[:, 0:LE], wmask, sem=s_vm)
        p.wait("tensor", s_vm)
        for lt in range(17):
            a = lt * 128
            w = min(128, LE - a)
            p.guard("tensor", ("psT", 0))
            tr(psT[0:w, 0:EFF], w8[:, a:a + w], sem=s_tm)
            p.guard("vector", ("wrow",))
            p.wait("vector", s_tm)
            vo("tensor_copy", wkB[0:w, 0:EFF], psT[0:w, 0:EFF], sem=s_vm)
            p.release(("psT", 0), s_vm)
            p.wait("scalar", s_vm)
            p.dma("scalar", w_hbm[a:a + w, :], wkB[0:w, 0:EFF], s_sd)
            p.release(("wrow",), s_sd)
        for e in range(EFF):
            p.wait("tensor", s_vm)
            p.guard("scalar", ("srowst",))
            p.wait("scalar", s_vm)
            p.dma("scalar", srow, cum8[e:e + 1, :], s_sd)
            p.wait("tensor", s_sd)
            for i, (a, b) in enumerate(tl):
                pse = psA if i % 2 == 0 else psB
                p.guard("tensor", ("psAB", i % 2))
                mm(pse[:, 0:b - a], ones_r_s, srow[0:1, a:b], True, True,
                   sem=s_tm)
                p.wait("vector", s_tm)
                vo("tensor_copy", cnt_b[:, a:b], pse[:, 0:b - a], sem=s_vm)
                p.release(("psAB", i % 2), s_vm)
            p.release(("srowst",), s_tm)
            for qc in range(QCH):
                col = e * QCH + qc
                vo("tensor_scalar", maskLE, cnt_b, qiota_s[:, qc:qc + 1],
                   None, OP.is_le)
                vo("tensor_tensor_scan", scanLE, onesLE, maskLE,
                   cst[:, 3:4], OP.mult, OP.add, sem=s_vm)
                p.wait("vector", s_vm)
                vo("tensor_copy", idxf[:, col:col + 1],
                   scanLE[:, LE - 1:LE],
                   sem=(s_vm if qc == QCH - 1 else None))
        p.wait("vector", s_vm)
        vo("tensor_copy", idxi, idxf, sem=s_vm)
        p.wait("gpsimd", s_vm)
        p.wait("gpsimd", s_sd)
        for e in range(EFF):
            for qc in range(QCH):
                col = e * QCH + qc
                ind_dma(wslot[:, col * 8:(col + 1) * 8], None, w_hbm,
                        IndirectOffsetOnAxis(idxi[:, col:col + 1], 0))

        if ffn_cut <= 2:
            return
        barrier()
        if DBG_FFN:
            p.dma("gpsimd", dbg["idxf"], idxf, s_out)
            p.dma("gpsimd", dbg["wsl"], wslot, s_out)
            p.dma("gpsimd", dbg["wh"], w_hbm, s_out)
            p.dma("gpsimd", dbg["xntm"], xn_tm, s_out)
            p.dma("gpsimd", dbg["cum8"], cum8, s_out)
            p.dma("gpsimd", dbg["sloc"], sloc, s_out)
            p.dma("gpsimd", dbg["bcol"], bcol, s_out)
        for e in range(EFF):
            p.guard("gpsimd", ("xg_tm",))
            for qc in range(QCH):
                col = e * QCH + qc
                ind_dma(xg_tm[:, qc * 1024:(qc + 1) * 1024], None, xn_tm,
                        IndirectOffsetOnAxis(idxi[:, col:col + 1], 0))
            if DBG_FFN and e == 0:
                p.dma("gpsimd", dbg["xgtm"], xg_tm, s_out)
            p.wait("tensor", s_gd)
            p.guard("tensor", ("xg_dm",))
            for qc in range(QCH):
                for c in range(DCH):
                    slot = c % 4
                    p.guard("tensor", ("psT", slot))
                    tr(psR[slot][:, 0:128],
                       xg_tm[:, qc * 1024 + c * 128:qc * 1024 + (c + 1) * 128],
                       sem=s_tm)
                    p.wait("scalar", s_tm)
                    scp(xg_dm[:, c * CPAD + qc * 128:
                              c * CPAD + (qc + 1) * 128],
                        psR[slot][:, 0:128], sem=s_sm)
                    p.release(("psT", slot), s_sm)
            p.release(("xg_tm",), s_sm)
            if DBG_FFN and e == 0:
                p.wait("gpsimd", s_sm)
                p.dma("gpsimd", dbg["xgdm"], xg_dm, s_out)
            p.wait("tensor", s_sm)
            for wi, wt_hbm in ((0, W1T), (1, W3T)):
                for ms in range(4):
                    for k in range(DCH):
                        p.guard("sync", ("wgt1", k % 2))
                        wb = wgt1[:, (k % 2) * NT:(k % 2) * NT + NT]
                        p.dma("sync", wb,
                              wt_hbm[e, k * 128:(k + 1) * 128,
                                     ms * 512:(ms + 1) * 512], s_ld)
                        p.wait("tensor", s_ld)
                        for m in range(4):
                            mm(psC[m][:, 0:CPAD],
                               wb[:, m * 128:(m + 1) * 128],
                               xg_dm[:, k * CPAD:(k + 1) * CPAD],
                               k == 0, k == DCH - 1,
                               sem=(s_tm if m == 3 else None))
                        p.release(("wgt1", k % 2), s_tm)
                    if wi == 0:
                        p.wait("scalar", s_tm)
                        for m in range(4):
                            mc = ms * 4 + m
                            act(h1s[:, mc * CPAD:(mc + 1) * CPAD],
                                psC[m][:, 0:CPAD], AF.Silu, sem=s_sm)
                        p.wait("tensor", s_sm)
                    else:
                        p.wait("vector", s_tm)
                        for m in range(4):
                            mc = ms * 4 + m
                            vo("scalar_tensor_tensor",
                               hp[:, mc * CPAD:(mc + 1) * CPAD],
                               psC[m][:, 0:CPAD], 1.0,
                               h1s[:, mc * CPAD:(mc + 1) * CPAD],
                               OP.mult, OP.mult, sem=s_vm)
                        p.wait("tensor", s_vm)
            if DBG_FFN and e == 0:
                p.wait("gpsimd", s_sm)
                p.wait("gpsimd", s_vm)
                p.dma("gpsimd", dbg["h1s"], h1s, s_out)
                p.dma("gpsimd", dbg["hp"], hp, s_out)
            allps = psC + [psA, psB, psT]
            for kh in range(HCH):
                p.guard("sync", ("wgt3", kh % 2))
                wb = wgt3[:, (kh % 2) * 1024:(kh % 2) * 1024 + 1024]
                p.dma("sync", wb, W2T[e, kh * 128:(kh + 1) * 128, :], s_ld)
                p.wait("tensor", s_ld)
                for cm in range(DCH):
                    mm(allps[cm][:, 0:CPAD], wb[:, cm * 128:(cm + 1) * 128],
                       hp[:, kh * CPAD:(kh + 1) * CPAD],
                       kh == 0, kh == HCH - 1,
                       sem=(s_tm if cm == DCH - 1 else None))
                p.release(("wgt3", kh % 2), s_tm)
            p.wait("vector", s_tm)
            for cm in range(DCH):
                vo("tensor_copy", y_dm[:, cm * CPAD:(cm + 1) * CPAD],
                   allps[cm][:, 0:CPAD],
                   sem=(s_vm if cm == DCH - 1 else None))
            p.wait("tensor", s_vm)
            for qc in range(QCH):
                for c in range(DCH):
                    slot = c % 4
                    p.guard("tensor", ("psT", slot))
                    tr(psR[slot][:, 0:128],
                       y_dm[:, c * CPAD + qc * 128:c * CPAD + (qc + 1) * 128],
                       sem=s_tm)
                    p.wait("vector", s_tm)
                    vo("tensor_copy",
                       y_tm[:, qc * 1024 + c * 128:qc * 1024 + (c + 1) * 128],
                       psR[slot][:, 0:128], sem=s_vm)
                    p.release(("psT", slot), s_vm)
                col = e * QCH + qc
                vo("tensor_scalar_mul", y_tm[:, qc * 1024:(qc + 1) * 1024],
                   y_tm[:, qc * 1024:(qc + 1) * 1024],
                   wslot[:, col * 8 + e:col * 8 + e + 1], sem=s_vm)
            p.wait("gpsimd", s_vm)
            if DBG_FFN and e == 0:
                p.dma("gpsimd", dbg["ydm"], y_dm, s_out)
                p.dma("gpsimd", dbg["ytm"], y_tm, s_out)
            for qc in range(QCH):
                col = e * QCH + qc
                p.wait("gpsimd", s_gd)
                ind_dma(acc_tm, IndirectOffsetOnAxis(idxi[:, col:col + 1], 0),
                        y_tm[:, qc * 1024:(qc + 1) * 1024], None, add=True)
            p.release(("y_work",), s_gd)

        if ffn_cut <= 3:
            return
        barrier()
        for lt in range(17):
            a = lt * 128
            w = min(128, LE - a)
            tb = (wkA if lt % 2 == 0 else wkB)[:, 0:1024]
            p.guard("sync", ("accrow", lt % 2))
            p.dma("sync", tb[0:w, :], acc_tm[a:a + w, :], s_ld)
            p.wait("tensor", s_ld)
            for c in range(DCH):
                slot = c % 4
                p.guard("tensor", ("psT", slot))
                tr(psR[slot][0:128, 0:w],
                   tb[0:w, c * 128:(c + 1) * 128], sem=s_tm)
                p.wait("vector", s_tm)
                vo("tensor_copy", xnA[c][:, a:a + w],
                   psR[slot][:, 0:w], sem=s_vm)
                p.release(("psT", slot), s_vm)
            p.release(("accrow", lt % 2), s_vm)
        p.release(("acc",), s_vm)
        for c in range(DCH):
            p.guard("sync", ("wkE",))
            p.dma("sync", wkE, x_hbm[c * 128:(c + 1) * 128, :], s_ld)
            p.wait("vector", s_ld)
            vo("tensor_add", xnA[c], xnA[c], wkE, sem=s_vm)
            p.release(("wkE",), s_vm)
            p.wait("gpsimd", s_vm)
            if last:
                p.dma("gpsimd", out_x[c * 128:(c + 1) * 128, :],
                      xnA[c][:, HALO:LE], s_out)
            else:
                p.dma("gpsimd", x_hbm[c * 128:(c + 1) * 128, :], xnA[c],
                      s_gd)
            p.release(("xnA", c), s_gd, s_out)

    def mamba_layer(li):
        sst = 3 + 3 * li
        cut = 2050 + 3 * li
        p.dma("sync", nrmw_s, nrmw[li], s_ld)
        p.dma("sync", convw_s, convw[li], s_ld)
        p.dma("sync", convb_s, convb[li], s_ld)
        p.dma("sync", Dp_s, Dp_c[li], s_ld)
        p.dma("sync", Alog_s, Alog_c[li], s_ld)
        p.dma("sync", bdt_s, bdt_c[li], s_ld)
        ln_stats_apply(xa_hbm, rms_li=li)

        p.wait("sync", s_vm)
        p.dma("sync", WdtT_s, WdtT[li], s_ld)
        p.dma("sync", WxT_s.rearrange("pp (m w) -> pp m w", m=ECH),
              WxT[li].rearrange("(m pp) w -> m pp w", pp=128)
              .transpose([1, 0, 2]), s_ld)
        p.dma("sync", mask_b, maskrow.broadcast_to([128, LE]), s_ld)
        p.wait("scalar", s_ld)
        act(nA_s, Alog_s, AF.Exp, sem=s_sm)
        p.wait("vector", s_sm)
        p.wait("vector", s_ld)
        vo("tensor_scalar_mul", A_s, nA_s, -1.0)
        vo("tensor_scalar_mul", nbdt, bdt_s, -1.0)
        vo("memset", convB[:, 0:3], 0.0, sem=s_vm)

        tl = ltiles(0, LE)
        p.wait("tensor", s_vm)
        for m in range(32):
            p.guard("sync", ("wgt3", m % 2))
            wb = wgt3[:, (m % 2) * 1024:(m % 2) * 1024 + 1024]
            p.dma("sync", wb.rearrange("pp (k w) -> pp k w", k=DCH),
                  WinT[li, :, m * 128:(m + 1) * 128]
                  .rearrange("(k pp) w -> k pp w", pp=128)
                  .transpose([1, 0, 2]), s_ld)
            p.wait("tensor", s_ld)
            pse = psA if m % 2 == 0 else psB
            for i, (a, b) in enumerate(tl):
                for k in range(DCH):
                    mm(pse[:, 0:b - a], wb[:, k * 128:(k + 1) * 128],
                       xnA[k][:, a:b], k == 0, k == DCH - 1,
                       sem=(s_tm if k == DCH - 1 else None))
                p.wait("scalar", s_tm)
                if m < 16:
                    scp(convB[:, 3 + a:3 + b], pse[:, 0:b - a], sem=s_sm)
                else:
                    p.guard("scalar", ("zrowb",)) if i == 0 else None
                    act(wkB[:, a:b], pse[:, 0:b - a], AF.Silu, sem=s_sm)
                p.wait("tensor", s_sm)
            p.release(("wgt3", m % 2), s_sm)
            if m < 16:
                p.guard("vector", ("convout",))
                p.wait("vector", s_sm)
                vo("tensor_scalar_mul", wkC, convB[:, 0:LE],
                   convw_s[:, m * 4:m * 4 + 1])
                for k in range(1, 4):
                    vo("scalar_tensor_tensor", wkC, convB[:, k:k + LE],
                       convw_s[:, m * 4 + k:m * 4 + k + 1],
                       wkC, OP.mult, OP.add,
                       sem=(s_vm if k == 3 else None))
                p.wait("scalar", s_vm)
                p.guard("scalar", ("xcout",))
                act(wkD, wkC, AF.Silu, bias=convb_s[:, m:m + 1], sem=s_sm)
                p.release(("convout",), s_sm)
                p.wait("gpsimd", s_sm)
                p.dma("gpsimd", xc_hbm[m * 128:(m + 1) * 128, :], wkD, s_gd)
                p.wait("tensor", s_sm)
                for i, (a, b) in enumerate(tl):
                    mm(psC[i][0:96, 0:b - a],
                       WxT_s[:, m * 96:(m + 1) * 96], wkD[:, a:b],
                       m == 0, m == 15,
                       sem=(s_tm if (m == 15 and i == len(tl) - 1)
                            else None))
                p.release(("xcout",), s_tm, s_gd)
            else:
                p.wait("gpsimd", s_sm)
                p.dma("gpsimd", sz_hbm[(m - 16) * 128:(m - 15) * 128, :],
                      wkB[:, 0:LE], s_gd)
                p.release(("zrowb",), s_gd)
        p.wait("vector", s_tm)
        for i, (a, b) in enumerate(tl):
            vo("tensor_copy", dtb[:, a:b], psC[i][0:RK, 0:b - a])
            vo("tensor_copy", BCr[:, a:b], psC[i][RK:96, 0:b - a],
               sem=(s_vm if i == len(tl) - 1 else None))
        p.wait("scalar", s_vm)
        p.dma("scalar", bc_hbm, BCr, s_sd)
        p.wait("tensor", s_vm)
        for m in range(ECH):
            p.guard("scalar", ("dlA",))
            for i, (a, b) in enumerate(tl):
                pse = psA if (m + i) % 2 == 0 else psB
                p.guard("tensor", ("psAB2", (m + i) % 2))
                mm(pse[:, 0:b - a], WdtT_s[:, m * 128:(m + 1) * 128],
                   dtb[:, a:b], True, True, sem=s_tm)
                p.wait("scalar", s_tm)
                act(wkA[:, a:b], pse[:, 0:b - a], AF.Sigmoid,
                    scale=cst[:, 2:3], bias=nbdt[:, m:m + 1], sem=s_sm)
                p.release(("psAB2", (m + i) % 2), s_sm)
            act(wkA, wkA, AF.Ln, sem=s_sm)
            p.wait("vector", s_sm)
            p.guard("vector", ("dlB",))
            vo("tensor_mul", wkB, wkA, mask_b, sem=s_vm)
            p.release(("dlA",), s_vm)
            p.wait("gpsimd", s_vm)
            p.dma("gpsimd", dl_hbm[m * 128:(m + 1) * 128, :], wkB, s_gd)
            p.release(("dlB",), s_gd)

        p.wait("vector", s_tm)
        p.wait("vector", s_sm)
        vo("memset", onesbig, 1.0, sem=s_vm)
        for h in range(2):
            ha, hb = (0, LH) if h == 0 else (LH, LE)
            w = hb - ha
            lo = max(sst, ha) - ha
            hi_ = (min(cut, hb - 1) - ha) + 1
            p.guard("scalar", ("bbc",))
            for n in range(NST):
                dst = (big1 if n < 8 else big2)
                p.dma("scalar", dst[:, (n % 8) * LH:(n % 8) * LH + w],
                      bc_hbm[n:n + 1, ha:hb].broadcast_to([128, w]), s_sd)
            p.wait("vector", s_sd)
            p.wait("sync", s_gd)
            for g in range(ECH):
                p.guard("sync", ("gAB", g % 2))
                dl = wkA[:, (g % 2) * LH:(g % 2) * LH + w]
                xc = wkB[:, (g % 2) * LH:(g % 2) * LH + w]
                p.dma("sync", dl, dl_hbm[g * 128:(g + 1) * 128, ha:hb], s_ld)
                p.dma("sync", xc, xc_hbm[g * 128:(g + 1) * 128, ha:hb], s_ld)
                p.wait("vector", s_ld)
                ini = 0.0 if h == 0 else scol[:, g * 4 + 1:g * 4 + 2]
                p.guard("vector", ("Sg",))
                vo("tensor_tensor_scan", wkC[:, 0:w], onesbig[:, 0:w], dl,
                   ini, OP.mult, OP.add)
                p.guard("vector", ("ug",))
                vo("scalar_tensor_tensor", wkD[:, 0:w], dl, -1.0, xc,
                   OP.mult, OP.mult, sem=s_vm)
                p.release(("gAB", g % 2), s_vm)
                p.wait("scalar", s_vm)
                if h == 0:
                    scp(scol[:, g * 4:g * 4 + 1], wkC[:, sst - 1:sst])
                    scp(scol[:, g * 4 + 1:g * 4 + 2], wkC[:, w - 1:w],
                        sem=s_sm)
                    refc = scol[:, g * 4 + 1:g * 4 + 2]
                else:
                    scp(scol[:, g * 4 + 2:g * 4 + 3],
                        wkC[:, cut - ha:cut - ha + 1], sem=s_sm)
                    refc = scol[:, g * 4 + 2:g * 4 + 3]
                p.wait("vector", s_sm)
                vo("tensor_scalar_mul", tiny[:, 0:1], refc, -1.0, sem=s_vm)
                p.wait("vector", s_vm)
                vo("tensor_scalar_mul", tiny[:, 1:17],
                   A_s[:, g * NST:(g + 1) * NST], tiny[:, 0:1], sem=s_vm)
                p.wait("scalar", s_vm)
                for n in range(NST):
                    p.guard("scalar", ("Ebuf",))
                    act(wkE[:, 0:w], wkC[:, 0:w], AF.Exp,
                        scale=A_s[:, g * NST + n:g * NST + n + 1],
                        bias=tiny[:, n + 1:n + 2], sem=s_sm)
                    p.wait("gpsimd", s_sm)
                    p.wait("gpsimd", s_vm)
                    go("tensor_mul", wkE[:, 0:w], wkE[:, 0:w], wkD[:, 0:w],
                       sem=s_gm)
                    p.wait("vector", s_gm)
                    bsl = (big1 if n < 8 else big2)[:, (n % 8) * LH + lo:
                                                    (n % 8) * LH + hi_]
                    dstp = (sumB0 if h == 0 else sumB1)
                    vo("scalar_tensor_tensor", wkE[:, lo:hi_],
                       wkE[:, lo:hi_], 1.0, bsl, OP.mult, OP.mult,
                       accum_out=dstp[:, g * NST + n:g * NST + n + 1],
                       sem=s_vm)
                    p.release(("Ebuf",), s_vm)
                    p.wait("scalar", s_vm)
                p.release(("Sg",), s_vm, s_sm)
                p.release(("ug",), s_vm, s_gm)
            p.release(("bbc",), s_vm, s_gm)
        p.wait("vector", s_sm)
        for g in range(ECH):
            vo("tensor_sub", tiny[:, g:g + 1],
               scol[:, g * 4 + 1:g * 4 + 2], scol[:, g * 4 + 2:g * 4 + 3])
            vo("tensor_sub", tiny[:, 16 + g:17 + g],
               scol[:, g * 4:g * 4 + 1], scol[:, g * 4 + 2:g * 4 + 3],
               sem=(s_vm if g == ECH - 1 else None))
        p.wait("vector", s_vm)
        for g in range(ECH):
            vo("tensor_scalar_mul", htmp[:, g * NST:(g + 1) * NST],
               A_s[:, g * NST:(g + 1) * NST], tiny[:, g:g + 1])
            vo("tensor_scalar_mul",
               htmp[:, 256 + g * NST:256 + (g + 1) * NST],
               A_s[:, g * NST:(g + 1) * NST], tiny[:, 16 + g:17 + g],
               sem=(s_vm if g == ECH - 1 else None))
        p.wait("scalar", s_vm)
        act(htmp[:, 0:256], htmp[:, 0:256], AF.Exp)
        act(sumA, htmp[:, 256:512], AF.Exp, sem=s_sm)
        p.wait("vector", s_sm)
        vo("tensor_mul", sumB0, sumB0, htmp[:, 0:256], sem=s_vm)
        p.wait("vector", s_vm)
        vo("tensor_add", sumB0, sumB0, sumB1, sem=s_vm)
        p.wait("scalar", s_vm)
        p.dma("scalar", cr_ag_in[:, 0:256], sumB0, s_sd)
        p.dma("scalar", cr_ag_in[:, 256:512], sumA, s_sd)
        p.wait("gpsimd", s_sd)
        s_cc.n += 1
        p.emit("gpsimd", lambda e, h=s_cc.h: e.collective_compute(
            "AllGather", OP.bypass, replica_groups=[list(range(NCORE))],
            ins=[cr_ag_in], outs=[cr_ag_out]).then_inc(h, 1))
        p.wait("vector", s_cc)
        p.wait("sync", s_cc)
        vo("memset", h_in, 0.0, sem=s_vm)
        p.wait("vector", s_vm)
        for j in range(NCORE - 1):
            p.guard("sync", ("crt",))
            p.dma("sync", crtile, cr_ag_out[j * 128:(j + 1) * 128, :], s_ld)
            p.wait("vector", s_ld)
            vo("tensor_mul", htmp[:, 0:256], crtile[:, 256:512], h_in,
               sem=s_vm)
            p.wait("vector", s_vm)
            vo("tensor_add", htmp[:, 0:256], htmp[:, 0:256],
               crtile[:, 0:256], sem=s_vm)
            p.release(("crt",), s_vm)
            p.wait("vector", s_vm)
            vo("tensor_sub", htmp[:, 256:512], htmp[:, 0:256], h_in,
               sem=s_vm)
            p.wait("vector", s_vm)
            vo("scalar_tensor_tensor", h_in, htmp[:, 256:512],
               fold_s[:, j:j + 1], h_in, OP.mult, OP.add, sem=s_vm)
            p.wait("vector", s_vm)

        for h in range(2):
            ha, hb = (0, LH) if h == 0 else (LH, LE)
            w = hb - ha
            lo = max(sst, ha) - ha
            for ns in range(2):
                p.guard("scalar", ("bbc",))
                for n8 in range(8):
                    n = ns * 8 + n8
                    p.dma("scalar", big1[:, n8 * LH:n8 * LH + w],
                          bc_hbm[n:n + 1, ha:hb].broadcast_to([128, w]),
                          s_sd)
                    p.dma("scalar", big2[:, n8 * LH:n8 * LH + w],
                          bc_hbm[16 + n:17 + n, ha:hb].broadcast_to([128, w]),
                          s_sd)
                p.wait("vector", s_sd)
                p.wait("gpsimd", s_sd)
                p.wait("sync", s_gd)
                for g in range(ECH):
                    p.guard("sync", ("gAB", g % 2))
                    dl = wkA[:, (g % 2) * LH:(g % 2) * LH + w]
                    xc = wkB[:, (g % 2) * LH:(g % 2) * LH + w]
                    p.dma("sync", dl, dl_hbm[g * 128:(g + 1) * 128, ha:hb],
                          s_ld)
                    p.dma("sync", xc, xc_hbm[g * 128:(g + 1) * 128, ha:hb],
                          s_ld)
                    p.wait("vector", s_ld)
                    p.wait("scalar", s_ld)
                    p.guard("vector", ("ug",))
                    vo("scalar_tensor_tensor", wkD[:, 0:w], dl, -1.0, xc,
                       OP.mult, OP.mult, sem=s_vm)
                    ybuf = (big3 if g < 8 else big4)[:, (g % 8) * LH:
                                                     (g % 8) * LH + w]
                    for n8 in range(8):
                        n = ns * 8 + n8
                        pc = g * NST + n
                        p.guard("scalar", ("dAb",))
                        act(wkC[:, lo:w], dl[:, lo:w], AF.Exp,
                            scale=nA_s[:, pc:pc + 1], sem=s_sm)
                        p.guard("gpsimd", ("BXb",))
                        p.wait("gpsimd", s_vm)
                        go("tensor_mul", wkE[:, lo:w], wkD[:, lo:w],
                           big1[:, n8 * LH + lo:n8 * LH + w], sem=s_gm)
                        p.wait("vector", s_gm)
                        p.wait("vector", s_sm)
                        ini = (h_in[:, pc:pc + 1] if h == 0
                               else stt_c[:, pc:pc + 1])
                        vo("tensor_tensor_scan", wkE[:, lo:w],
                           wkC[:, lo:w], wkE[:, lo:w], ini,
                           OP.mult, OP.add, sem=s_vm)
                        if h == 0:
                            p.wait("scalar", s_vm)
                            scp(stt_c[:, pc:pc + 1], wkE[:, w - 1:w],
                                sem=s_sm)
                        if n == 0:
                            vo("scalar_tensor_tensor", ybuf[:, lo:w],
                               wkE[:, lo:w], 1.0,
                               big2[:, n8 * LH + lo:n8 * LH + w],
                               OP.mult, OP.mult, sem=s_vm)
                        else:
                            vo("scalar_tensor_tensor", wkC[:, lo:w],
                               wkE[:, lo:w], 1.0,
                               big2[:, n8 * LH + lo:n8 * LH + w],
                               OP.mult, OP.mult)
                            vo("tensor_add", ybuf[:, lo:w], ybuf[:, lo:w],
                               wkC[:, lo:w], sem=s_vm)
                        p.release(("dAb",), s_vm)
                        if h == 0:
                            p.release(("BXb",), s_vm, s_sm)
                        else:
                            p.release(("BXb",), s_vm)
                    p.release(("ug",), s_vm, s_gm)
                    p.release(("gAB", g % 2), s_vm, s_sm, s_gm)
            p.wait("sync", s_gd)
            for g in range(ECH):
                p.guard("sync", ("gAB", g % 2))
                xc = wkB[:, (g % 2) * LH:(g % 2) * LH + w]
                sz = wkA[:, (g % 2) * LH:(g % 2) * LH + w]
                p.dma("sync", xc, xc_hbm[g * 128:(g + 1) * 128, ha:hb], s_ld)
                p.dma("sync", sz, sz_hbm[g * 128:(g + 1) * 128, ha:hb], s_ld)
                ybuf = (big3 if g < 8 else big4)[:, (g % 8) * LH:
                                                 (g % 8) * LH + w]
                p.wait("vector", s_ld)
                vo("scalar_tensor_tensor", ybuf[:, lo:w], xc[:, lo:w],
                   Dp_s[:, g:g + 1], ybuf[:, lo:w], OP.mult, OP.add)
                vo("tensor_mul", ybuf[:, lo:w], ybuf[:, lo:w], sz[:, lo:w],
                   sem=s_vm)
                p.release(("gAB", g % 2), s_vm)
            if DBG_MAMBA and li == 0:
                p.wait("gpsimd", s_vm)
                p.dma("gpsimd", dbg[f"yz3h{h}"], big3, s_out)
                p.dma("gpsimd", dbg[f"yz4h{h}"], big4, s_out)
                p.wait("sync", s_out)
            p.wait("sync", s_vm)
            wtl = ltiles(ha + lo, hb)
            for dch in range(DCH):
                p.guard("sync", ("wout", dch % 2))
                wb = (wkC if dch % 2 == 0 else wkD)[:, 0:2048]
                p.dma("sync", wb.rearrange("pp (g w) -> pp g w", g=ECH),
                      WoutT[li, :, dch * 128:(dch + 1) * 128]
                      .rearrange("(g pp) w -> g pp w", pp=128)
                      .transpose([1, 0, 2]), s_ld)
                p.wait("tensor", s_ld)
                for i, (a, b) in enumerate(wtl):
                    ps = psA if (dch + i) % 2 == 0 else psB
                    p.guard("tensor", ("psAB3", (dch + i) % 2))
                    for g in range(ECH):
                        ybuf = (big3 if g < 8 else big4)
                        mm(ps[:, 0:b - a], wb[:, g * 128:(g + 1) * 128],
                           ybuf[:, (g % 8) * LH + (a - ha):
                                (g % 8) * LH + (b - ha)],
                           g == 0, g == ECH - 1,
                           sem=(s_tm if g == ECH - 1 else None))
                    p.guard("sync", ("xarow",))
                    xb = wkE[:, 0:b - a]
                    p.dma("sync", xb, xa_hbm[dch * 128:(dch + 1) * 128, a:b],
                          s_ld)
                    p.wait("vector", s_ld)
                    p.wait("vector", s_tm)
                    vo("tensor_add", xb, xb, ps[:, 0:b - a], sem=s_vm)
                    p.release(("psAB3", (dch + i) % 2), s_vm)
                    p.wait("gpsimd", s_vm)
                    p.dma("gpsimd", xa_hbm[dch * 128:(dch + 1) * 128, a:b],
                          xb, s_gd)
                    p.release(("xarow",), s_gd)
                p.release(("wout", dch % 2), s_tm)
            barrier()

    zero_acc()
    ffn_sublayer(first=True, last=(stage == 1 and not tap_aps and ffn_cut >= 99))
    if "x1" in tap_aps:
        barrier()
        p.dma("gpsimd", tap_aps["x1"], x_hbm, s_out)
    if stage >= 2:
        ln_stats_apply(x_hbm)
        p.wait("gpsimd", s_vm)
        for c in range(DCH):
            p.dma("gpsimd", xa_hbm[c * 128:(c + 1) * 128, :], xnA[c], s_gd)
            p.release(("xnA", c), s_gd)
        barrier()
        if "xa0" in tap_aps:
            p.dma("gpsimd", tap_aps["xa0"], xa_hbm, s_out)
            barrier()
        mamba_layer(0)
        barrier()
        if "xa1" in tap_aps:
            p.dma("gpsimd", tap_aps["xa1"], xa_hbm, s_out)
            barrier()
        if nmamba > 1:
            mamba_layer(1)
        barrier()
        if "xa2" in tap_aps:
            p.dma("gpsimd", tap_aps["xa2"], xa_hbm, s_out)
            barrier()
    if stage >= 3:
        zero_acc()
        ffn_sublayer(first=False, last=True)
    elif stage == 2 or (stage == 1 and (tap_aps or ffn_cut < 99)):
        barrier()
        for c in range(DCH):
            p.guard("sync", ("fin",))
            p.dma("sync", wkA, x_hbm[c * 128:(c + 1) * 128, :], s_ld)
            if stage >= 2:
                p.dma("sync", wkB, xa_hbm[c * 128:(c + 1) * 128, :], s_ld)
            p.wait("vector", s_ld)
            if stage >= 2:
                vo("tensor_add", wkA, wkA, wkB, sem=s_vm)
            else:
                vo("tensor_scalar_mul", wkA, wkA, 1.0, sem=s_vm)
            p.wait("gpsimd", s_vm)
            p.dma("gpsimd", out_x[c * 128:(c + 1) * 128, :], wkA[:, HALO:LE],
                  s_out)
            p.release(("fin",), s_out)

    barrier()
    p.wait("gpsimd", s_out)
    p.wait("sync", s_out)

    with nc.Block() as block:
        @block.sync
        def _(sync):
            for fn in p.ops["sync"]:
                fn(sync)

        @block.tensor
        def _(tensor):
            for fn in p.ops["tensor"]:
                fn(tensor)

        @block.vector
        def _(vector):
            for fn in p.ops["vector"]:
                fn(vector)

        @block.scalar
        def _(scalar):
            for fn in p.ops["scalar"]:
                fn(scalar)

        @block.gpsimd
        def _(gpsimd):
            for fn in p.ops["gpsimd"]:
                fn(gpsimd)

    return nc


def _prep(inputs):
    f = np.float32
    x = np.ascontiguousarray(np.asarray(inputs["L0_attn_x"], f)[0])
    Wg = np.asarray(inputs["L0_ffn_Wg"], f)
    W1 = np.asarray(inputs["L0_ffn_W1"], f)
    W2 = np.asarray(inputs["L0_ffn_W2"], f)
    W3 = np.asarray(inputs["L0_ffn_W3"], f)
    norm_w = np.asarray(inputs["L0_attn_norm_w"], f)
    Win = np.asarray(inputs["L0_attn_Win"], f)
    conv_w = np.asarray(inputs["L0_attn_conv_w"], f)
    conv_b = np.asarray(inputs["L0_attn_conv_b"], f)
    Wx = np.asarray(inputs["L0_attn_Wx"], f)
    Wdt = np.asarray(inputs["L0_attn_Wdt"], f)
    bdt = np.asarray(inputs["L0_attn_bdt"], f)
    A_log = np.asarray(inputs["L0_attn_A_log"], f)
    Dp = np.asarray(inputs["L0_attn_Dp"], f)
    Wout = np.asarray(inputs["L0_attn_Wout"], f)

    com = {
        "bd16": np.kron(np.eye(8, dtype=f), np.ones((16, 16), f)),
        "sel16": np.kron(np.eye(8, dtype=f), np.ones((16, 1), f))
        * (np.arange(128)[:, None] % 16 == 0),
        "ones_r": np.ones((1, 128), f),
        "ones_c": np.ones((128, 1), f),
        "qiota": (np.arange(128, dtype=f)[:, None]
                  + 128.0 * np.arange(QCH, dtype=f)[None, :]).astype(f),
        "idmat": np.eye(128, dtype=f),
        "WgT": np.ascontiguousarray(
            Wg.T.reshape(DCH, 128, EFF).transpose(1, 0, 2)
            .reshape(128, DCH * EFF)),
        "W1T": np.ascontiguousarray(W1.transpose(0, 2, 1)),
        "W3T": np.ascontiguousarray(W3.transpose(0, 2, 1)),
        "W2T": np.ascontiguousarray(W2.transpose(0, 2, 1)),
        "nrmw": np.ascontiguousarray(
            norm_w.reshape(2, DCH, 128).transpose(0, 2, 1)),
        "WinT": np.ascontiguousarray(Win.transpose(0, 2, 1)),
        "convw": np.ascontiguousarray(
            conv_w.reshape(2, ECH, 128, 4).transpose(0, 2, 1, 3)
            .reshape(2, 128, ECH * 4)),
        "convb": np.ascontiguousarray(
            conv_b.reshape(2, ECH, 128).transpose(0, 2, 1)),
        "WxT": np.ascontiguousarray(Wx.transpose(0, 2, 1)),
        "WdtT": np.ascontiguousarray(Wdt.transpose(0, 2, 1)),
        "bdt_c": np.ascontiguousarray(
            bdt.reshape(2, ECH, 128).transpose(0, 2, 1)),
        "Alog_c": np.ascontiguousarray(
            A_log.reshape(2, ECH, 128, NST).transpose(0, 2, 1, 3)
            .reshape(2, 128, ECH * NST)),
        "Dp_c": np.ascontiguousarray(
            Dp.reshape(2, ECH, 128).transpose(0, 2, 1)),
        "WoutT": np.ascontiguousarray(Wout.transpose(0, 2, 1)),
    }
    in_maps = []
    for c in range(NCORE):
        if c == 0:
            sh = np.concatenate([np.zeros((HALO, D), f), x[:L_OWN]], 0)
            mk = np.concatenate([np.zeros(HALO, f), np.ones(L_OWN, f)])
        else:
            sh = x[c * L_OWN - HALO:(c + 1) * L_OWN]
            mk = np.ones(LE, f)
        m = dict(com)
        m["x_in"] = np.ascontiguousarray(sh.T)
        m["maskrow"] = np.ascontiguousarray(mk[None, :])
        fm = np.zeros((128, 8), f)
        fm[:, :c] = 1.0
        m["foldmask"] = fm
        in_maps.append(m)
    return in_maps


def kernel(**inputs):
    in_maps = _prep(inputs)
    nc = build_program(stage=3)
    res = run_bass_kernel_spmd(nc, in_maps, core_ids=list(range(NCORE)))
    outs = [np.asarray(r["out_x"]) for r in res.results]
    full = np.concatenate([o.T for o in outs], axis=0)
    return np.ascontiguousarray(full[None, :, :]).astype(np.float32)
